# version 01
# baseline (speedup 1.0000x reference)
import numpy as np
import ml_dtypes
from contextlib import ExitStack

import concourse.bass as bass
import concourse.mybir as mybir
from concourse.bass_utils import run_bass_kernel_spmd

F32 = mybir.dt.float32
BF16 = mybir.dt.bfloat16
AF = mybir.ActivationFunctionType
ALU = mybir.AluOpType
AX = mybir.AxisListType

MAGIC = 1.5 * 2.0 ** 23
EXP_BIAS = -2.0
EPS = 1e-5
D, T, H, HD, FF = 1024, 2048, 16, 64, 4096
NT, NSEQ, NCORES = 16, 2, 8
DEBUG = False


class Sem:
    def __init__(self, nc, ctx, name):
        self.h = ctx.enter_context(nc.semaphore(name))
        self.v = 0


class Emitter:
    ENGS = ("tensor", "vector", "scalar", "gpsimd", "sync")

    def __init__(self):
        self.ops = {k: [] for k in self.ENGS}
        self.vec_sem = None
        self.sca_sem = None
        self.flush_dst_v = None
        self.flush_dst_s = None

    def wait(self, eng, *toks):
        toks = tuple(t for t in toks if t is not None)
        if not toks:
            return
        def fn(E, toks=toks):
            for t in toks:
                E.wait(t)
        self.ops[eng].append(fn)

    def op(self, eng, f, inc=None, n=1, flush_ap=None):
        if eng in ("vector", "scalar") and self.vec_sem is not None:
            sem = inc if inc is not None else self.vec_sem
            sem.v += n if inc is not None else 1
            tok = (sem, sem.v)
            def fn(E, f=f, sem=sem, v=tok[1]):
                f(E.e).then_inc(sem.h, v - (sem_prev[0] if False else 0) if False else (n if False else 1))
            def fn(E, f=f, sem=sem, nn=(n if inc is not None else 1)):
                f(E.e).then_inc(sem.h, nn)
            self.ops[eng].append(fn)
            self.wait(eng, tok)
            return tok
        if inc is None:
            self.ops[eng].append(lambda E, f=f: f(E.e))
            return None
        inc.v += n
        tok = (inc, inc.v)
        def fn(E, f=f, inc=inc, n=n):
            f(E.e).then_inc(inc.h, n)
        self.ops[eng].append(fn)
        return tok

    def dma(self, eng, sem, out, in_, **kw):
        sem.v += 16
        tok = (sem, sem.v)
        def fn(E, out=out, in_=in_, kw=kw, sem=sem):
            E.e.dma_start(out=out, in_=in_, **kw).then_inc(sem.h, 16)
        self.ops[eng].append(fn)
        return tok

    def raw(self, eng, f):
        self.ops[eng].append(lambda E, f=f: f(E.e))


class EngCtx:
    def __init__(self, e):
        self.e = e
        self.seen = {}

    def wait(self, tok):
        sem, val = tok
        if val <= 0 or self.seen.get(id(sem), -1) >= val:
            return
        self.e.wait_ge(sem.h, val)
        self.seen[id(sem)] = val


def build_program():
    nc = bass.Bass("TRN2", target_bir_lowering=False)
    ctx = ExitStack()
    nc._kernel_ctx = ctx
    dbg_tensors = {}

    x_in = nc.dram_tensor("x", [NSEQ, T, D], F32, kind="ExternalInput").ap()
    w_in = {n: nc.dram_tensor(n, [D, D], F32, kind="ExternalInput").ap()
            for n in ("wq", "wk", "wv", "wo")}
    w1_in = nc.dram_tensor("w1", [FF, D], F32, kind="ExternalInput").ap()
    b1_in = nc.dram_tensor("b1", [1, FF], F32, kind="ExternalInput").ap()
    w2_in = nc.dram_tensor("w2", [D, FF], F32, kind="ExternalInput").ap()
    b2_in = nc.dram_tensor("b2", [1, D], F32, kind="ExternalInput").ap()
    cos_in = nc.dram_tensor("cos_t", [T, 32], F32, kind="ExternalInput").ap()
    sin_in = nc.dram_tensor("sin_t", [T, 32], F32, kind="ExternalInput").ap()
    id_in = nc.dram_tensor("ident", [128, 128], BF16, kind="ExternalInput").ap()
    shf_in = nc.dram_tensor("shf", [64, 128], BF16, kind="ExternalInput").ap()
    cst_in = nc.dram_tensor("cst", [128, 8], F32, kind="ExternalInput").ap()
    y_out = nc.dram_tensor("y", [NSEQ, T, D], F32, kind="ExternalOutput").ap()
    w_p = {n: nc.dram_tensor(n + "_p", [8, 128, D], BF16, kind="Internal").ap()
           for n in ("wq", "wk", "wv", "wo")}
    w1_p = nc.dram_tensor("w1_p", [8, 128, FF], BF16, kind="Internal").ap()
    w2_p = nc.dram_tensor("w2_p", [32, 128, D], BF16, kind="Internal").ap()
    scl_p = nc.dram_tensor("scl_p", [4, 1], F32, kind="Internal").ap()
    xq_p = nc.dram_tensor("xq_p", [8, 128, T], BF16, kind="Internal").ap()

    def arena(name, nbytes):
        return ctx.enter_context(nc.sbuf_tensor(name, [128, nbytes // 4], F32))

    AR_P = arena("arP", 17408)
    AR_B = arena("arB", 98816)
    AR_C = arena("arC", 49152)
    AR_D = arena("arD", 46080)

    def carve(ar, off, shape, dtype):
        nel = int(np.prod(shape[1:]))
        assert off % 4 == 0
        if dtype == BF16:
            assert nel % 2 == 0
            v = ar[:][0:shape[0], off // 4: off // 4 + nel // 2].bitcast(BF16)
        else:
            v = ar[:][0:shape[0], off // 4: off // 4 + nel]
        if len(shape) > 2:
            pat = " ".join(f"d{i}" for i in range(1, len(shape)))
            v = v.rearrange(f"p ({pat}) -> p {pat}",
                            **{f"d{i}": shape[i] for i in range(1, len(shape))})
        return v

    cos_tok = carve(AR_P, 0, [128, NT, 32], F32)
    sin_tok = carve(AR_P, 2048, [128, NT, 32], F32)
    cstf = carve(AR_P, 4096, [128, 8], F32)
    ident = carve(AR_P, 4128, [128, 128], BF16)
    shfI = carve(AR_P, 4384, [64, 128], BF16)
    ones_bf = carve(AR_P, 4640, [4, 128], BF16)
    onesrow_f = carve(AR_P, 4896, [1, 128], F32)
    onescol_f = carve(AR_P, 5408, [128, 1], F32)
    onescol_b = carve(AR_P, 5412, [128, 2], BF16)
    b1rows = carve(AR_P, 5440, [2, FF], BF16)
    b2rows = carve(AR_P, 13632, [2, D], BF16)
    STATS = carve(AR_P, 15680, [128, 176], F32)
    _so = [0]
    def stat(n=1):
        o = _so[0]; _so[0] += n
        assert _so[0] <= 176
        return STATS[:, o:o + n]

    QT = carve(AR_B, 0, [128, 8, T], BF16)
    KT = carve(AR_B, 32768, [128, 8, T], BF16)
    Vones = carve(AR_B, 65536, [128, NT, H, 65], BF16)
    gF = [carve(AR_B, 0, [128, FF], F32), carve(AR_B, 16384, [128, FF], F32)]
    hqF = carve(AR_B, 32768, [128, FF], BF16)
    hqT = [carve(AR_B, 40960, [128, 32, 128], BF16),
           carve(AR_B, 49152, [128, 32, 128], BF16)]
    xqF = [carve(AR_B, 57344, [128, 8, 128], BF16),
           carve(AR_B, 59392, [128, 8, 128], BF16)]
    xF = [carve(AR_B, 61440, [128, D], F32), carve(AR_B, 65536, [128, D], F32)]
    yF = [carve(AR_B, 69632, [128, D], F32), carve(AR_B, 73728, [128, D], F32)]
    sqscrapF = carve(AR_B, 77824, [128, 512], F32)
    wbufP = [carve(AR_B, 0, [128, 4096], F32), carve(AR_B, 16384, [128, 4096], F32)]
    wtmpP = carve(AR_B, 32768, [128, 4096], F32)
    woutP = carve(AR_B, 49152, [128, 4096], BF16)
    toutP = carve(AR_B, 57344, [128, 32, 128], BF16)
    browP = carve(AR_B, 65536, [1, FF], F32)
    blo_bf = carve(AR_B, 84992, [1, FF], BF16)
    absaccP = carve(AR_B, 81920, [128, 256], F32)
    red16P = carve(AR_B, 83968, [128, 16], F32)

    wqkv_sb = {n: carve(AR_C, i * 16384, [128, 8, D], BF16)
               for i, n in enumerate(("wq", "wk", "wv"))}
    PTb = [[carve(AR_C, (h * 2 + p) * 2048, [128, 1024], BF16) for p in range(2)]
           for h in range(2)]
    arecip = carve(AR_C, 8192, [1, 512], F32)
    arhi = carve(AR_C, 10240, [1, 512], BF16)
    arlo = carve(AR_C, 11264, [1, 512], BF16)
    arlof = carve(AR_C, 12288, [1, 512], F32)
    atmp = carve(AR_C, 14336, [64, 512], F32)
    atmpbf = carve(AR_C, 16384, [64, 512], BF16)
    w1db = [carve(AR_C, 0, [128, 8, 512], BF16), carve(AR_C, 8192, [128, 8, 512], BF16)]
    wo_sb = carve(AR_C, 16384, [128, 8, D], BF16)
    w2db = [carve(AR_C, 32768, [128, D], BF16), carve(AR_C, 34816, [128, D], BF16)]
    xbE = [carve(AR_D, 0, [128, D], F32), carve(AR_D, 4096, [128, D], F32)]
    tmpE = [carve(AR_D, 8192, [128, D], F32), carve(AR_D, 12288, [128, D], F32)]
    xnE = carve(AR_D, 16384, [128, D], F32)
    QfE = carve(AR_D, 20480, [128, D], F32)
    KfE = carve(AR_D, 24576, [128, D], F32)
    QrE = carve(AR_D, 28672, [128, D], BF16)
    KrE = carve(AR_D, 30720, [128, D], BF16)
    xnbfE = carve(AR_D, 32768, [128, D], BF16)
    xqbfE = carve(AR_D, 34816, [128, D], BF16)
    xnTE = [carve(AR_D, 36864, [128, 8, 128], BF16),
            carve(AR_D, 38912, [128, 8, 128], BF16)]
    tA = carve(AR_D, 40960, [128, 16, 32], F32)
    tB = carve(AR_D, 43008, [128, 16, 32], F32)
    attnT = carve(AR_D, 0, [128, 8, T], BF16)

    PS = ctx.enter_context(nc.psum_tensor("ps", [128, 4096], F32))
    def psv(off, shape, dtype=F32):
        w = shape[1] if dtype == F32 else shape[1] // 2
        v = PS[:][0:shape[0], off: off + w]
        return v.bitcast(BF16) if dtype == BF16 else v
    psQ, psK, psV = psv(0, [128, 1024]), psv(1024, [128, 1024]), psv(2048, [128, 1024])
    psStg = [psv(3072, [128, 1024], BF16), psv(3584, [128, 1024], BF16)]
    psPool = [psv(0, [128, 1024]), psv(1024, [128, 1024]), psv(2048, [128, 1024])]
    psPV = [psv(3072, [65, 512]), psv(3584, [65, 512])]
    psRep = psv(0, [64, 512])
    psShift = psv(512, [128, 512])
    psH = [psv(0, [128, 512]), psv(512, [128, 512]), psv(1024, [128, 512])]
    psStgF = [psv(0, [128, 1024], BF16), psv(512, [128, 1024], BF16)]
    psY = [psv(1536, [128, 1024]), psv(2560, [128, 1024])]

    nc.const_aps.register(F32, EXP_BIAS, cstf[:, 0:1])
    nc.const_aps.register(F32, MAGIC, cstf[:, 1:2])

    sSY = Sem(nc, ctx, "sSY"); sGQ = Sem(nc, ctx, "sGQ")
    sPE = Sem(nc, ctx, "sPE"); sDV = Sem(nc, ctx, "sDV")
    sAC = Sem(nc, ctx, "sAC"); sGP = Sem(nc, ctx, "sGP")
    ALLSEMS = (sSY, sGQ, sPE, sDV, sAC, sGP)

    EM = Emitter()
    EM.vec_sem = sDV
    EM.sca_sem = sAC
    FLS = carve(AR_P, 16704, [1, 4], F32)
    EM.flush_dst_v = FLS[0:1, 0:1]
    EM.flush_dst_s = FLS[0:1, 1:2]
    W, O, DMA = EM.wait, EM.op, EM.dma

    def DUMP(name, ap, shape, dtype, *toks):
        if not DEBUG or name in dbg_tensors:
            return
        dt_ = nc.dram_tensor("dbg_" + name, list(shape), dtype, kind="ExternalOutput").ap()
        dbg_tensors[name] = dt_
        W("gpsimd", *toks)
        tok = DMA("gpsimd", sGQ, dt_, ap)
        for eng in Emitter.ENGS:
            W(eng, tok)

    def snap():
        return [(s, s.v) for s in ALLSEMS]

    DMA("sync", sSY, cos_tok, cos_in.rearrange("(i p) c -> p i c", p=128))
    DMA("sync", sSY, sin_tok, sin_in.rearrange("(i p) c -> p i c", p=128))
    DMA("sync", sSY, cstf, cst_in)
    DMA("sync", sSY, ident, id_in)
    DMA("sync", sSY, shfI, shf_in)
    t_b1 = DMA("sync", sSY, browP, b1_in)
    t_cst = (sSY, sSY.v)

    W("vector", t_cst)
    O("vector", lambda e: e.memset(ones_bf[:], 1.0))
    O("vector", lambda e: e.memset(onesrow_f[:], 1.0))
    O("vector", lambda e: e.memset(onescol_f[:], 1.0))
    O("vector", lambda e: e.memset(onescol_b[:, 0:1], 1.0))

    W("vector", t_b1)
    O("vector", lambda e: e.tensor_copy(b1rows[0:1, :], browP[:]))
    O("vector", lambda e: e.tensor_sub(browP[:], browP[:], b1rows[0:1, :]))
    t_b1l = O("vector", lambda e: e.tensor_copy(blo_bf[:], browP[:]), inc=sDV, flush_ap=blo_bf[0:1, 0:1])
    W("gpsimd", t_b1l)
    t_b1r = DMA("gpsimd", sGQ, b1rows[1:2, :], blo_bf[:])
    W("sync", t_b1l)
    t_b2 = DMA("sync", sSY, browP[0:1, 0:D], b2_in)
    W("vector", t_b2, t_b1r)
    O("vector", lambda e: e.tensor_copy(b2rows[0:1, :], browP[0:1, 0:D]))
    O("vector", lambda e: e.tensor_sub(browP[0:1, 0:D], browP[0:1, 0:D], b2rows[0:1, :]))
    t_b2l = O("vector", lambda e: e.tensor_copy(blo_bf[0:1, 0:D], browP[0:1, 0:D]), inc=sDV, flush_ap=blo_bf[0:1, 0:1])
    W("gpsimd", t_b2l)
    t_b2r = DMA("gpsimd", sGQ, b2rows[1:2, :], blo_bf[0:1, 0:D])

    last_wout = [None, None]
    wbuf_cons = [None, None]
    for mi, name in enumerate(("wq", "wk", "wv", "wo")):
        for dc in range(8):
            par = (mi * 8 + dc) % 2
            dsthalf = woutP[:, par * 2048 + 0: par * 2048 + D]
            W("sync", wbuf_cons[par])
            tl = DMA("sync", sSY, wbufP[par][:, 0:D],
                     w_in[name][dc * 128:(dc + 1) * 128, :])
            W("vector", tl, last_wout[par])
            tc = O("vector", lambda e, par=par, dsthalf=dsthalf:
                   e.tensor_copy(dsthalf, wbufP[par][:, 0:D]), inc=sDV)
            wbuf_cons[par] = tc
            W("gpsimd", tc)
            last_wout[par] = DMA("gpsimd", sGQ, w_p[name][dc], dsthalf)

    prevl = [None, None]
    for fc in range(32):
        par = fc % 2
        W("sync", wbuf_cons[par])
        tl = DMA("sync", sSY, wbufP[par][:, 0:D], w1_in[fc * 128:(fc + 1) * 128, :])
        W("vector", tl, prevl[par], last_wout[par])
        last_wout[par] = None
        prevl[par] = O("vector", lambda e, fc=fc, par=par: e.tensor_reduce(
            absaccP[:, fc * 8:(fc + 1) * 8].unsqueeze(2),
            wbufP[par][:, 0:D].rearrange("p (a b) -> p a b", a=8), AX.X, ALU.add,
            apply_absolute_value=True), inc=sDV)
        wbuf_cons[par] = prevl[par]
    sum1 = stat(); sum_lo1 = stat()
    O("vector", lambda e: e.tensor_reduce(red16P[:].unsqueeze(2),
                                          absaccP[:].rearrange("p (a b) -> p a b", a=16),
                                          AX.X, ALU.add))
    t_s1 = O("vector", lambda e: e.tensor_reduce(sum1, red16P[:], AX.X, ALU.add), inc=sDV)
    DUMP("absacc1", absaccP[:], [128, 256], F32, t_s1)
    for dc in range(8):
        par = dc % 2
        W("sync", wbuf_cons[par])
        tl = DMA("sync", sSY, wbufP[par][:], w2_in[dc * 128:(dc + 1) * 128, :])
        W("vector", tl, prevl[par])
        prevl[par] = O("vector", lambda e, dc=dc, par=par: e.tensor_reduce(
            absaccP[:, dc * 8:(dc + 1) * 8].unsqueeze(2),
            wbufP[par][:].rearrange("p (a b) -> p a b", a=8), AX.X, ALU.add,
            apply_absolute_value=True), inc=sDV)
        wbuf_cons[par] = prevl[par]
    sum2 = stat()
    O("vector", lambda e: e.tensor_reduce(red16P[:, 0:8].unsqueeze(2),
                                          absaccP[:, 0:64].rearrange("p (a b) -> p a b", a=8),
                                          AX.X, ALU.add))
    t_s2 = O("vector", lambda e: e.tensor_reduce(sum2, red16P[:, 0:8], AX.X, ALU.add), inc=sDV)
    DUMP("absacc2", absaccP[:], [128, 256], F32, t_s2)
    DUMP("wbuf_w2last", wbufP[1][:], [128, 4096], F32, t_s2)

    s1hi = carve(AR_P, 16384, [128, 4], BF16)
    O("vector", lambda e: e.tensor_copy(s1hi[:, 0:1], sum1))
    O("vector", lambda e: e.tensor_sub(sum_lo1, sum1, s1hi[:, 0:1]))
    O("vector", lambda e: e.tensor_copy(s1hi[:, 1:2], sum_lo1))
    O("vector", lambda e: e.tensor_copy(s1hi[:, 2:3], sum2))
    O("vector", lambda e: e.tensor_sub(sum_lo1, sum2, s1hi[:, 2:3]))
    t_hilo = O("vector", lambda e: e.tensor_copy(s1hi[:, 3:4], sum_lo1), inc=sDV)
    W("tensor", t_s1, t_s2, t_hilo)
    O("tensor", lambda e: e.matmul(PS[0:1, 0:1], s1hi[:, 0:1], onescol_b[:, 0:1], start=True, stop=False))
    O("tensor", lambda e: e.matmul(PS[0:1, 0:1], s1hi[:, 1:2], onescol_b[:, 0:1], start=False, stop=True))
    O("tensor", lambda e: e.matmul(PS[0:1, 512:513], s1hi[:, 2:3], onescol_b[:, 0:1], start=True, stop=False))
    t_m = O("tensor", lambda e: e.matmul(PS[0:1, 512:513], s1hi[:, 3:4], onescol_b[:, 0:1], start=False, stop=True), inc=sPE)
    dqw1v, dqw2v, sw1v, sw2v = stat(), stat(), stat(), stat()
    W("vector", t_m)
    O("vector", lambda e: e.tensor_scalar(dqw1v[0:1, :], PS[0:1, 0:1], 1.0 / (FF * D), 1e-5, ALU.mult, ALU.max))
    O("vector", lambda e: e.tensor_scalar(dqw2v[0:1, :], PS[0:1, 512:513], 1.0 / (FF * D), 1e-5, ALU.mult, ALU.max))
    O("vector", lambda e: e.reciprocal(sw1v[0:1, :], dqw1v[0:1, :]))
    O("vector", lambda e: e.reciprocal(sw2v[0:1, :], dqw2v[0:1, :]))
    nr1, nr2 = stat(), stat()
    for (sv, dv_) in ((sw1v, dqw1v), (sw2v, dqw2v)):
        O("vector", lambda e, sv=sv, dv_=dv_: e.tensor_mul(nr1[0:1, :], dv_[0:1, :], sv[0:1, :]))
        O("vector", lambda e: e.tensor_scalar(nr1[0:1, :], nr1[0:1, :], -1.0, 2.0, ALU.mult, ALU.add))
        O("vector", lambda e, sv=sv: e.tensor_mul(sv[0:1, :], sv[0:1, :], nr1[0:1, :]))
    t_sc = O("vector", lambda e: e.tensor_copy(nr2[0:1, :], nr1[0:1, :]), inc=sDV)
    dqw1, dqw2, sw1, sw2 = stat(), stat(), stat(), stat()
    W("gpsimd", t_sc)
    t_up = None
    for idx, srcv in enumerate((dqw1v, dqw2v, sw1v, sw2v)):
        t_up = DMA("gpsimd", sGQ, scl_p[idx], srcv[0:1, :])
    W("gpsimd", t_up)
    for idx, dstv in enumerate((dqw1, dqw2, sw1, sw2)):
        t_screp = DMA("gpsimd", sGQ, dstv, scl_p[idx].to_broadcast((128, 1)))
    DUMP("prostats", STATS[:, 0:16], [128, 16], F32, t_screp)

    def quant_chunk(par, width, sw_ap, dq_ap):
        O("vector", lambda e, par=par, width=width, sw_ap=sw_ap: e.tensor_scalar(
            wtmpP[:, 0:width], wbufP[par][:, 0:width], sw_ap, MAGIC, ALU.mult, ALU.add))
        O("vector", lambda e, width=width: e.tensor_scalar(
            wtmpP[:, 0:width], wtmpP[:, 0:width], -MAGIC, -1.0, ALU.add, ALU.max))
        return O("vector", lambda e, width=width, dq_ap=dq_ap: e.tensor_scalar(
            woutP[:, 0:width], wtmpP[:, 0:width], 1.0, dq_ap, ALU.min, ALU.mult), inc=sDV, flush_ap=woutP[0:1, 0:1])

    prevl = [None, None]
    stage_cons = [None, None]
    prev_quant = None
    prev_tout_dma = None
    for fc in range(32):
        par = fc % 2
        W("sync", wbuf_cons[par])
        tl = DMA("sync", sSY, wbufP[par][:, 0:D], w1_in[fc * 128:(fc + 1) * 128, :])
        W("vector", tl, t_screp, prevl[par], prev_quant)
        tq = quant_chunk(par, D, sw1, dqw1)
        if fc == 0:
            DUMP("w1q0", woutP[:, 0:D], [128, D], BF16, tq)
        prevl[par] = tq
        wbuf_cons[par] = tq
        prev_quant = None
        sp = fc % 2
        W("tensor", tq, stage_cons[sp])
        for dc in range(8):
            last = (dc == 7)
            tok = O("tensor", lambda e, dc=dc, sp=sp: e.transpose(
                psStg[sp][:, dc * 128:(dc + 1) * 128],
                woutP[:, dc * 128:(dc + 1) * 128], ident),
                inc=sPE if last else None)
        t_tr = tok
        W("vector", t_tr, prev_tout_dma)
        t_ev = O("vector", lambda e, sp=sp: e.tensor_copy(
            toutP[:, 0:8, :], psStg[sp][:].rearrange("p (a b) -> p a b", a=8)), inc=sDV, flush_ap=toutP[0:1, 0:1, 0:1])
        stage_cons[sp] = t_ev
        prev_quant = t_tr
        W("gpsimd", t_ev)
        prev_tout_dma = DMA("gpsimd", sGQ, w1_p[:, :, fc * 128:(fc + 1) * 128].rearrange("a p b -> p a b"),
                            toutP[:, 0:8, :])
    for dc in range(8):
        par = dc % 2
        W("sync", wbuf_cons[par])
        tl = DMA("sync", sSY, wbufP[par][:], w2_in[dc * 128:(dc + 1) * 128, :])
        W("vector", tl, prevl[par], prev_quant)
        tq = quant_chunk(par, FF, sw2, dqw2)
        prevl[par] = tq
        wbuf_cons[par] = tq
        t_ev = None
        for r in range(4):
            sp = r % 2
            W("tensor", tq, stage_cons[sp])
            for b in range(8):
                fcb = r * 8 + b
                last = (b == 7)
                tok = O("tensor", lambda e, fcb=fcb, b=b, sp=sp: e.transpose(
                    psStg[sp][:, b * 128:(b + 1) * 128],
                    woutP[:, fcb * 128:(fcb + 1) * 128], ident),
                    inc=sPE if last else None)
            W("vector", tok, prev_tout_dma if r == 0 else None)
            t_ev = O("vector", lambda e, sp=sp, r=r: e.tensor_copy(
                toutP[:, r * 8:(r + 1) * 8, :],
                psStg[sp][:].rearrange("p (a b) -> p a b", a=8)), inc=sDV)
            stage_cons[sp] = t_ev
        prev_quant = tok
        W("gpsimd", t_ev)
        prev_tout_dma = DMA("gpsimd", sGQ, w2_p[:, :, dc * 128:(dc + 1) * 128].rearrange("a p b -> p a b"),
                            toutP[:])
    t_pro_gq = (sGQ, sGQ.v)

    cE = {k: [stat(), stat()] for k in
          ("mean", "var", "rstd", "c2", "nm", "amax", "f1", "dq1", "t0", "t1", "sq")}
    cF = {k: stat() for k in ("ms", "a", "s", "rc", "am2", "d2", "f2", "dq2", "u0", "u1")}
    cF["sqc"] = [stat(8), stat(8)]

    y_last = [None]

    def emit_E_tile(s, i, st8):
        par = i % 2
        W("gpsimd", st8["xb_free"][par])
        t_x = DMA("gpsimd", sGQ, xbE[par], x_in[s, i * 128:(i + 1) * 128, :])

        W("vector", t_x)
        O("vector", lambda e, par=par: e.tensor_reduce(
            cE["t0"][par], xbE[par][:], AX.X, ALU.add))
        W("scalar", t_x, st8["tmp_free"][par])
        t_sq = O("scalar", lambda e, par=par: e.activation(
            tmpE[par][:], xbE[par][:], AF.Square, accum_out=cE["sq"][par]), inc=sAC,
            flush_ap=cE["sq"][par][0:1, 0:1])
        W("vector", t_sq)
        O("vector", lambda e, par=par: e.tensor_scalar(cE["mean"][par], cE["t0"][par], 1.0 / D, None, ALU.mult))
        O("vector", lambda e, par=par: e.tensor_mul(cE["t0"][par], cE["mean"][par], cE["mean"][par]))
        O("vector", lambda e, par=par: e.tensor_scalar(cE["t1"][par], cE["sq"][par], 1.0 / D, None, ALU.mult))
        O("vector", lambda e, par=par: e.tensor_sub(cE["var"][par], cE["t1"][par], cE["t0"][par]))
        t_va = O("vector", lambda e, par=par: e.tensor_scalar(
            cE["t0"][par], cE["var"][par], EPS, None, ALU.add), inc=sDV)
        W("scalar", t_va)
        t_sq1 = O("scalar", lambda e, par=par: e.activation(cE["t1"][par], cE["t0"][par], AF.Sqrt), inc=sAC,
            flush_ap=cE["t1"][par][0:1, 0:1])
        W("vector", t_sq1)
        O("vector", lambda e, par=par: e.reciprocal(cE["rstd"][par], cE["t1"][par]))
        O("vector", lambda e, par=par: e.reciprocal(cE["t1"][par], cE["t0"][par]))
        O("vector", lambda e, par=par: e.tensor_mul(cE["t0"][par], cE["var"][par], cE["t1"][par]))
        t_vb = O("vector", lambda e, par=par: e.tensor_scalar(
            cE["t0"][par], cE["t0"][par], EPS, None, ALU.add), inc=sDV)
        W("scalar", t_vb)
        t_sq2 = O("scalar", lambda e, par=par: e.activation(cE["t1"][par], cE["t0"][par], AF.Sqrt), inc=sAC,
            flush_ap=cE["t1"][par][0:1, 0:1])
        W("vector", t_sq2)
        O("vector", lambda e, par=par: e.reciprocal(cE["c2"][par], cE["t1"][par]))
        O("vector", lambda e, par=par: e.tensor_scalar(cE["nm"][par], cE["mean"][par], -1.0, None, ALU.mult))
        t_xn = O("vector", lambda e, par=par: e.tensor_scalar(
            xnE[:], xbE[par][:], cE["nm"][par], cE["rstd"][par], ALU.add, ALU.mult), inc=sDV)
        st8["xb_free"][par] = t_xn
        W("vector", st8["xnbf_free"])
        t_xnbf = O("vector", lambda e: e.tensor_copy(xnbfE[:], xnE[:]), inc=sDV, flush_ap=xnbfE[0:1, 0:1])
        O("vector", lambda e, par=par: e.tensor_reduce(
            cE["amax"][par], xnE[:], AX.X, ALU.max, apply_absolute_value=True))
        O("vector", lambda e, par=par: e.tensor_mul(cE["t0"][par], cE["c2"][par], cE["amax"][par]))
        O("vector", lambda e, par=par: e.tensor_scalar(cE["t0"][par], cE["t0"][par], 1e-5, None, ALU.max))
        O("vector", lambda e, par=par: e.tensor_scalar(cE["dq1"][par], cE["t0"][par], 1.0 / 127.0, None, ALU.mult))
        O("vector", lambda e, par=par: e.reciprocal(cE["t1"][par], cE["t0"][par]))
        O("vector", lambda e, par=par: e.tensor_mul(cE["t1"][par], cE["c2"][par], cE["t1"][par]))
        O("vector", lambda e, par=par: e.tensor_scalar(cE["f1"][par], cE["t1"][par], 127.0, None, ALU.mult))
        O("vector", lambda e, par=par: e.tensor_scalar(
            tmpE[par][:], xnE[:], cE["f1"][par], MAGIC, ALU.mult, ALU.add))
        W("vector", st8["xqbf_free"])
        t_xq = O("vector", lambda e, par=par: e.tensor_scalar(
            xqbfE[:], tmpE[par][:], -MAGIC, cE["dq1"][par], ALU.add, ALU.mult), inc=sDV, flush_ap=xqbfE[0:1, 0:1])
        st8["tmp_free"][par] = t_xq
        if s == 0 and i == 0:
            DUMP("xn", xnE[:], [128, D], F32, t_xn)
            DUMP("xq", xqbfE[:], [128, D], BF16, t_xq)
            DUMP("estats", STATS[:, 16:48], [128, 32], F32, t_xq)

        W("tensor", t_xnbf, st8["stage_free"][0], st8["xnT_free"][par])
        for dc in range(8):
            tok = O("tensor", lambda e, dc=dc: e.transpose(
                psStg[0][:, dc * 128:(dc + 1) * 128], xnbfE[:, dc * 128:(dc + 1) * 128], ident),
                inc=sPE if dc == 7 else None)
        t_trxn = tok
        W("tensor", t_xq, st8["stage_free"][1])
        for dc in range(8):
            tok = O("tensor", lambda e, dc=dc: e.transpose(
                psStg[1][:, dc * 128:(dc + 1) * 128], xqbfE[:, dc * 128:(dc + 1) * 128], ident),
                inc=sPE if dc == 7 else None)
        t_trxq = tok
        st8["xqbf_free"] = t_trxq
        W("vector", t_trxn)
        t_xnT = O("vector", lambda e, par=par: e.tensor_copy(
            xnTE[par][:], psStg[0][:].rearrange("p (a b) -> p a b", a=8)), inc=sDV, flush_ap=xnTE[par][0:1, 0:1, 0:1])
        st8["stage_free"][0] = t_xnT
        W("vector", t_trxq, st8["xnbf_spilled"])
        t_xqT = O("vector", lambda e: e.tensor_copy(
            xnbfE[:].rearrange("p (a b) -> p a b", a=8),
            psStg[1][:].rearrange("p (a b) -> p a b", a=8)), inc=sDV, flush_ap=xnbfE[0:1, 0:1])
        st8["stage_free"][1] = t_xqT
        st8["xnbf_free"] = None
        W("gpsimd", t_xqT)
        t_sp = DMA("gpsimd", sGQ, xq_p[:, :, i * 128:(i + 1) * 128].rearrange("a p b -> p a b"),
                   xnbfE[:].rearrange("p (a b) -> p a b", a=8))
        st8["xq_spill"].append(t_sp)
        st8["xnbf_spilled"] = t_sp
        st8["xnbf_free"] = t_sp

        W("tensor", t_xnT, st8["t_w"], *st8["proj_psum_free"])
        for dc in range(8):
            st_, sp_ = (dc == 0), (dc == 7)
            for k, (ps_, nm) in enumerate(((psQ, "wq"), (psK, "wk"), (psV, "wv"))):
                for half in range(2):
                    last = sp_ and k == 2 and half == 1
                    tok = O("tensor", lambda e, dc=dc, ps_=ps_, nm=nm, half=half, st_=st_, sp_=sp_, par=par:
                            e.matmul(ps_[:, half * 512:(half + 1) * 512],
                                     xnTE[par][:, dc, :],
                                     wqkv_sb[nm][:, dc, half * 512:(half + 1) * 512],
                                     start=st_, stop=sp_),
                            inc=sPE if last else None)
        t_proj = tok
        st8["xnT_free"][par] = t_proj

        W("scalar", t_proj, *st8["rope_prev"])
        t_qf = O("scalar", lambda e: e.copy(QfE[:], psQ[:]), inc=sAC, flush_ap=QfE[0:1, 0:1])
        t_kf = O("scalar", lambda e: e.copy(KfE[:], psK[:]), inc=sAC, flush_ap=KfE[0:1, 0:1])
        W("vector", t_proj, st8["t_vset"])
        t_vev = O("vector", lambda e, i=i: e.tensor_copy(
            Vones[:, i, :, 0:64], psV[:].rearrange("p (h c) -> p h c", c=64)), inc=sDV, flush_ap=Vones[0:1, 0:1, 0:1, 0:1])
        st8["proj_psum_free"] = [t_qf, t_kf, t_vev]

        cosb = cos_tok[:, i, :].unsqueeze(1).to_broadcast((128, 16, 32))
        sinb = sin_tok[:, i, :].unsqueeze(1).to_broadcast((128, 16, 32))
        def rv(buf, parity):
            return buf.rearrange("p (h q t) -> p h q t", h=16, q=32)[:, :, :, parity]
        QE_, QO_ = rv(QfE[:], 0), rv(QfE[:], 1)
        KE_, KO_ = rv(KfE[:], 0), rv(KfE[:], 1)
        QrEv, QrOv = rv(QrE[:], 0), rv(QrE[:], 1)
        KrEv, KrOv = rv(KrE[:], 0), rv(KrE[:], 1)
        tAv, tBv = tA, tB
        W("vector", t_qf, st8["qr_free"])
        O("vector", lambda e, QE_=QE_, cosb=cosb, tAv=tAv: e.tensor_tensor(tAv, QE_, cosb, ALU.mult))
        O("vector", lambda e, QO_=QO_, sinb=sinb, tBv=tBv: e.tensor_tensor(tBv, QO_, sinb, ALU.mult))
        O("vector", lambda e, QrEv=QrEv, tAv=tAv, tBv=tBv: e.tensor_tensor(QrEv, tAv, tBv, ALU.subtract))
        O("vector", lambda e, QE_=QE_, sinb=sinb, tAv=tAv: e.tensor_tensor(tAv, QE_, sinb, ALU.mult))
        O("vector", lambda e, QO_=QO_, cosb=cosb, tBv=tBv: e.tensor_tensor(tBv, QO_, cosb, ALU.mult))
        t_ropeq = O("vector", lambda e, QrOv=QrOv, tAv=tAv, tBv=tBv: e.tensor_tensor(QrOv, tAv, tBv, ALU.add), inc=sDV, flush_ap=QrE[0:1, 0:1])
        W("gpsimd", t_kf, st8["kr_free"])
        O("gpsimd", lambda e, KrEv=KrEv, KE_=KE_, cosb=cosb: e.tensor_tensor(KrEv, KE_, cosb, ALU.mult))
        O("gpsimd", lambda e, KrOv=KrOv, KO_=KO_, sinb=sinb: e.tensor_tensor(KrOv, KO_, sinb, ALU.mult))
        O("gpsimd", lambda e, KrEv=KrEv, KrOv=KrOv: e.tensor_tensor(KrEv, KrEv, KrOv, ALU.subtract))
        O("gpsimd", lambda e, KrOv=KrOv, KO_=KO_, cosb=cosb: e.tensor_tensor(KrOv, KO_, cosb, ALU.mult))
        O("gpsimd", lambda e, KE_=KE_, sinb=sinb: e.tensor_tensor(KE_, KE_, sinb, ALU.mult))
        t_ropek = O("gpsimd", lambda e, KrOv=KrOv, KE_=KE_: e.tensor_tensor(KrOv, KrOv, KE_, ALU.add), inc=sGP)
        st8["rope_prev"] = [t_ropeq, t_ropek]
        if s == 0 and i == 0:
            DUMP("qf", QfE[:], [128, D], F32, t_qf)
            DUMP("qr", QrE[:], [128, D], BF16, t_ropeq)
            DUMP("kr", KrE[:], [128, D], BF16, t_ropek)

        W("tensor", t_ropeq, st8["stage_free"][0])
        for dc in range(8):
            tok = O("tensor", lambda e, dc=dc: e.transpose(
                psStg[0][:, dc * 128:(dc + 1) * 128], QrE[:, dc * 128:(dc + 1) * 128], ident),
                inc=sPE if dc == 7 else None)
        t_trq = tok
        st8["qr_free"] = t_trq
        W("tensor", t_ropek, st8["stage_free"][1])
        for dc in range(8):
            tok = O("tensor", lambda e, dc=dc: e.transpose(
                psStg[1][:, dc * 128:(dc + 1) * 128], KrE[:, dc * 128:(dc + 1) * 128], ident),
                inc=sPE if dc == 7 else None)
        t_trk = tok
        st8["kr_free"] = t_trk
        W("vector", t_trq)
        t_qtw = O("vector", lambda e, i=i: e.tensor_copy(
            QT[:, :, i * 128:(i + 1) * 128], psStg[0][:].rearrange("p (a b) -> p a b", a=8)), inc=sDV, flush_ap=QT[0:1, 0:1, 0:1])
        st8["stage_free"][0] = t_qtw
        W("vector", t_trk)
        t_ktw = O("vector", lambda e, i=i: e.tensor_copy(
            KT[:, :, i * 128:(i + 1) * 128], psStg[1][:].rearrange("p (a b) -> p a b", a=8)), inc=sDV, flush_ap=KT[0:1, 0:1, 0:1])
        st8["stage_free"][1] = t_ktw

    def emit_A_iter(qb, hp, stA):
        t_pv_ = None
        for j in range(8):
            p0, p1, pp = (2 * j) % 3, (2 * j + 1) % 3, j % 2
            W("tensor", stA["pool_free"][p0], stA["pool_free"][p1])
            for kc2 in range(2):
                kc = 2 * j + kc2
                for h in range(2):
                    pool = psPool[p0 if h == 0 else p1]
                    lo = 0 if h == 0 else 64
                    last = (kc2 == 1 and h == 1)
                    tok = O("tensor", lambda e, kc=kc, kc2=kc2, lo=lo, pool=pool, hp=hp, qb=qb:
                            e.matmul(pool[:, kc2 * 512:(kc2 + 1) * 512],
                                     KT[lo:lo + 64, hp, kc * 128:(kc + 1) * 128],
                                     QT[lo:lo + 64, hp, qb * 512:(qb + 1) * 512],
                                     start=True, stop=True, tile_position=(lo, 0)),
                            inc=sPE if last else None)
            t_sc_ = tok
            for h, pi in ((0, p0), (1, p1)):
                W("scalar", t_sc_, stA["pt_free"][h][pp])
                t_e = O("scalar", lambda e, h=h, pi=pi, pp=pp: e.activation(
                    PTb[h][pp][:], psPool[pi][:], AF.Exp, bias=EXP_BIAS, scale=0.125), inc=sAC,
                    flush_ap=PTb[h][pp][0:1, 0:1])
                stA["pool_free"][pi] = t_e
                stA["pt_exp"][h][pp] = t_e
                if h == 0 and j == 0:
                    DUMP("pt00", PTb[0][0][:], [128, 1024], BF16, t_e)
            st_ = (j == 0)
            W("tensor", stA["pt_exp"][0][pp], stA["pt_exp"][1][pp],
              stA["pv_free"] if st_ else None)
            for h in range(2):
                for kc2 in range(2):
                    kc = 2 * j + kc2
                    h_abs = 2 * hp + h
                    last = (h == 1 and kc2 == 1)
                    tok = O("tensor", lambda e, h=h, kc=kc, kc2=kc2, h_abs=h_abs, pp=pp, st_=st_:
                            e.matmul(psPV[h][:],
                                     Vones[:, kc, h_abs, :],
                                     PTb[h][pp][:, kc2 * 512:(kc2 + 1) * 512],
                                     start=(st_ and kc2 == 0), stop=(kc == 15 and kc2 == 1)),
                            inc=sPE if last else None)
            t_pv_ = tok
            stA["pt_free"][0][pp] = t_pv_
            stA["pt_free"][1][pp] = t_pv_
        t_an = None
        for h in range(2):
            W("vector", t_pv_, stA["atmp_free"])
            O("vector", lambda e, h=h: e.reciprocal(arecip[:], psPV[h][64:65, :]))
            O("vector", lambda e: e.tensor_copy(arhi[:], arecip[:]), flush_ap=arhi[0:1, 0:1])
            O("vector", lambda e: e.tensor_sub(arlof[:], arecip[:], arhi[:]))
            t_hl = O("vector", lambda e: e.tensor_copy(arlo[:], arlof[:]), inc=sDV, flush_ap=arlo[0:1, 0:1])
            W("tensor", t_hl, stA["pool_free"][0])
            O("tensor", lambda e: e.matmul(psRep[:], ones_bf[0:1, 0:64], arhi[:], start=True, stop=False))
            t_rp = O("tensor", lambda e: e.matmul(psRep[:], ones_bf[0:1, 0:64], arlo[:], start=False, stop=True), inc=sPE)
            W("scalar", t_pv_, stA["atmp_free"])
            t_at = O("scalar", lambda e, h=h: e.copy(atmp[:], psPV[h][0:64, :]), inc=sAC, flush_ap=atmp[0:1, 0:1])
            if h == 0:
                DUMP("recip00", arecip[:], [1, 512], F32, t_hl)
                DUMP("pv00", atmp[:], [64, 512], F32, t_at)
            if h == 0:
                W("vector", t_at, t_rp)
                t_an = O("vector", lambda e, hp=hp, qb=qb: e.tensor_mul(
                    attnT[0:64, hp, qb * 512:(qb + 1) * 512], atmp[:], psRep[:]), inc=sDV, flush_ap=attnT[0:1, hp, qb * 512:qb * 512 + 1])
            else:
                W("vector", t_at, t_rp, stA["atmpbf_free"])
                t_nb = O("vector", lambda e: e.tensor_mul(atmpbf[:], atmp[:], psRep[:]), inc=sDV, flush_ap=atmpbf[0:1, 0:1])
                W("tensor", t_nb, stA["pool_free"][0])
                t_sh = O("tensor", lambda e: e.matmul(
                    psShift[:], shfI[:], atmpbf[:], start=True, stop=True), inc=sPE)
                W("vector", t_sh)
                t_an = O("vector", lambda e, hp=hp, qb=qb: e.tensor_copy(
                    attnT[64:128, hp, qb * 512:(qb + 1) * 512], psShift[64:128, :]), inc=sDV, flush_ap=attnT[64:65, hp, qb * 512:qb * 512 + 1])
                stA["atmpbf_free"] = t_an
            stA["pool_free"][0] = t_an
            stA["atmp_free"] = t_an
        stA["pv_free"] = t_an
        stA["t_an"] = t_an

    def emit_F_group(s, g, stF):
        tiles = [2 * g, 2 * g + 1]
        for tp, i in enumerate(tiles):
            W("gpsimd", stF["xqF_used"][tp])
            stF["xqF"][tp] = DMA("gpsimd", sGQ, xqF[tp][:],
                                 xq_p[:, :, i * 128:(i + 1) * 128].rearrange("a p b -> p a b"))
            W("gpsimd", stF["xF_used"][tp])
            stF["xF"][tp] = DMA("gpsimd", sGQ, xF[tp], x_in[s, i * 128:(i + 1) * 128, :])
        t_gelu_last = [None, None]
        for fb in range(8):
            wpar = fb % 2
            W("sync", stF["w1_used"][wpar])
            t_w1 = DMA("sync", sSY, w1db[wpar][:],
                       w1_p[:, :, fb * 512:(fb + 1) * 512].rearrange("a p b -> p a b"))
            for tp in range(2):
                hb = (fb * 2 + tp) % 3
                W("tensor", t_w1, stF["xqF"][tp], stF["h_free"][hb],
                  stF["first_gate"])
                stF["first_gate"] = None
                for dc in range(8):
                    O("tensor", lambda e, dc=dc, tp=tp, wpar=wpar, hb=hb: e.matmul(
                        psH[hb][:], xqF[tp][:, dc, :], w1db[wpar][:, dc, :],
                        start=(dc == 0), stop=False))
                t_h = O("tensor", lambda e, hb=hb, fb=fb: e.matmul(
                    psH[hb][:], ones_bf[0:2, :], b1rows[:, fb * 512:(fb + 1) * 512],
                    start=False, stop=True), inc=sPE)
                stF["w1_used"][wpar] = t_h
                if s == 0 and g == 0 and fb == 3 and tp == 0:
                    DUMP("w1db3", w1db[wpar][:], [128, 8, 512], BF16, t_w1)
                    DUMP("xqF0", xqF[0][:], [128, 8, 128], BF16, stF["xqF"][0])
                    DUMP("b1r", b1rows[:], [2, FF], BF16, t_h)
                if s == 0 and g == 0 and fb == 2 and tp == 0:
                    DUMP("w1db2", w1db[wpar][:], [128, 8, 512], BF16, t_w1)
                    DUMP("g2early", gF[0][:, 1024:1536], [128, 512], F32, t_gl)
                W("scalar", t_h, stF["g_free"][tp] if fb == 0 else None)
                t_gl = O("scalar", lambda e, tp=tp, fb=fb, hb=hb: e.activation(
                    gF[tp][:, fb * 512:(fb + 1) * 512], psH[hb][:], AF.Gelu), inc=sAC)
                stF["h_free"][hb] = t_gl
                t_sq_ = O("scalar", lambda e, tp=tp, fb=fb: e.activation(
                    sqscrapF[:], gF[tp][:, fb * 512:(fb + 1) * 512], AF.Square,
                    accum_out=cF["sqc"][tp][:, fb:fb + 1]), inc=sAC,
                    flush_ap=cF["sqc"][tp][:, fb:fb + 1].tensor[0:1, 0:1] if False else cF["sqc"][tp][0:1, fb:fb + 1])
                t_gelu_last[tp] = t_sq_
        t_hqT_ready = [None, None]
        for tp in range(2):
            i = tiles[tp]
            W("vector", t_gelu_last[tp])
            O("vector", lambda e, tp=tp: e.tensor_reduce(
                cF["am2"], gF[tp][:], AX.X, ALU.max, apply_absolute_value=True))
            O("vector", lambda e, tp=tp: e.tensor_reduce(cF["ms"], cF["sqc"][tp][:], AX.X, ALU.add))
            O("vector", lambda e: e.tensor_scalar(cF["a"], cF["ms"], 1.0 / FF, EPS, ALU.mult, ALU.add))
            O("vector", lambda e: e.tensor_scalar(cF["s"], cF["a"], 0.5, 0.5, ALU.mult, ALU.add))
            for _ in range(6):
                O("vector", lambda e: e.reciprocal(cF["u0"], cF["s"]))
                O("vector", lambda e: e.tensor_mul(cF["u0"], cF["a"], cF["u0"]))
                O("vector", lambda e: e.tensor_add(cF["u1"], cF["u0"], cF["s"]))
                O("vector", lambda e: e.tensor_scalar(cF["s"], cF["u1"], 0.5, None, ALU.mult))
            O("vector", lambda e: e.reciprocal(cF["rc"], cF["s"]))
            O("vector", lambda e: e.tensor_mul(cF["u0"], cF["rc"], cF["am2"]))
            O("vector", lambda e: e.tensor_scalar(cF["d2"], cF["u0"], 1e-5, None, ALU.max))
            O("vector", lambda e: e.tensor_scalar(cF["dq2"], cF["d2"], 1.0 / 127.0, None, ALU.mult))
            O("vector", lambda e: e.reciprocal(cF["u0"], cF["d2"]))
            O("vector", lambda e: e.tensor_mul(cF["u0"], cF["rc"], cF["u0"]))
            t_f2 = O("vector", lambda e: e.tensor_scalar(cF["f2"], cF["u0"], 127.0, None, ALU.mult), inc=sDV)
            if s == 0 and g == 0 and tp == 0:
                DUMP("g0", gF[0][:], [128, FF], F32, t_gelu_last[0])
                DUMP("fstats", STATS[:, 48:80], [128, 32], F32, t_f2)
            W("scalar", t_f2)
            t_p1 = O("scalar", lambda e, tp=tp: e.activation(
                gF[tp][:], gF[tp][:], AF.Copy, bias=MAGIC, scale=cF["f2"]), inc=sAC, flush_ap=gF[tp][0:1, 0:1])
            W("vector", t_p1, stF["hq_free"])
            t_hq = O("vector", lambda e, tp=tp: e.tensor_scalar(
                hqF[:], gF[tp][:], -MAGIC, cF["dq2"], ALU.add, ALU.mult), inc=sDV, flush_ap=hqF[0:1, 0:1])
            stF["g_free"][tp] = t_hq
            if s == 0 and g == 0 and tp == 0:
                DUMP("hq0", hqF[:], [128, FF], BF16, t_hq)
            if s == 1 and g == 5 and tp == 0:
                DUMP("hqX", hqF[:], [128, FF], BF16, t_hq)
                DUMP("gX", gF[0][:], [128, FF], F32, t_f2)
                DUMP("fstX", STATS[:, 100:176], [128, 76], F32, t_f2)
            t_ev = None
            for r in range(4):
                sp = r % 2
                W("tensor", t_hq, stF["stg_free"][sp],
                  stF["h_free"][0], stF["h_free"][1], stF["h_free"][2])
                for b in range(8):
                    fcb = r * 8 + b
                    tok = O("tensor", lambda e, fcb=fcb, b=b, sp=sp: e.transpose(
                        psStgF[sp][:, b * 128:(b + 1) * 128],
                        hqF[:, fcb * 128:(fcb + 1) * 128], ident),
                        inc=sPE if b == 7 else None)
                W("vector", tok, stF["hqT_used"][tp] if r == 0 else None)
                t_ev = O("vector", lambda e, tp=tp, r=r, sp=sp: e.tensor_copy(
                    hqT[tp][:, r * 8:(r + 1) * 8, :],
                    psStgF[sp][:].rearrange("p (a b) -> p a b", a=8)), inc=sDV, flush_ap=hqT[tp][0:1, 0:1, 0:1])
                stF["stg_free"][sp] = t_ev
                stF["h_free"][sp] = t_ev
            stF["hq_free"] = t_ev
            t_hqT_ready[tp] = t_ev
            W("tensor", stF["t_wo"], stF["y_ps_free"][tp], stF["t_attnT"], t_b2r)
            for ob in range(2):
                for dc in range(8):
                    O("tensor", lambda e, tp=tp, dc=dc, ob=ob, i=i: e.matmul(
                        psY[tp][:, ob * 512:(ob + 1) * 512],
                        attnT[:, dc, i * 128:(i + 1) * 128],
                        wo_sb[:, dc, ob * 512:(ob + 1) * 512],
                        start=(dc == 0), stop=False))
            O("tensor", lambda e, tp=tp: e.matmul(
                psY[tp][:, 0:512], ones_bf[0:2, :], b2rows[:, 0:512], start=False, stop=False))
            O("tensor", lambda e, tp=tp: e.matmul(
                psY[tp][:, 512:1024], ones_bf[0:2, :], b2rows[:, 512:1024], start=False, stop=False))
        t_f2mm = None
        for fc in range(32):
            wpar = fc % 2
            W("sync", stF["w2_used"][wpar])
            t_w2 = DMA("sync", sSY, w2db[wpar], w2_p[fc])
            W("tensor", t_w2, t_hqT_ready[0] if fc == 0 else None,
              t_hqT_ready[1] if fc == 0 else None)
            for tp in range(2):
                for ob in range(2):
                    last = (tp == 1 and ob == 1)
                    tok = O("tensor", lambda e, tp=tp, fc=fc, ob=ob, wpar=wpar: e.matmul(
                        psY[tp][:, ob * 512:(ob + 1) * 512],
                        hqT[tp][:, fc, :], w2db[wpar][:, ob * 512:(ob + 1) * 512],
                        start=False, stop=(fc == 31)),
                        inc=sPE if last else None)
            t_f2mm = tok
            stF["w2_used"][wpar] = t_f2mm
        stF["hqT_used"][0] = t_f2mm
        stF["hqT_used"][1] = t_f2mm
        for tp in range(2):
            i = tiles[tp]
            W("vector", t_f2mm, stF["xF"][tp], stF["y_sb_free"][tp])
            t_y = O("vector", lambda e, tp=tp: e.tensor_add(yF[tp], xF[tp], psY[tp]), inc=sDV, flush_ap=yF[tp][0:1, 0:1])
            stF["y_ps_free"][tp] = t_y
            stF["xF_used"][tp] = t_y
            stF["xqF_used"][tp] = t_f2mm
            if s == 0 and g == 0 and tp == 0:
                DUMP("y0", yF[0][:], [128, D], F32, t_y)
            if s == 1 and g == 5 and tp == 0:
                DUMP("yX", yF[0][:], [128, D], F32, t_y)
                DUMP("xX", xF[0][:], [128, D], F32, t_y)
            W("gpsimd", t_y)
            t_st = DMA("gpsimd", sGQ, y_out[s, i * 128:(i + 1) * 128, :], yF[tp])
            stF["y_sb_free"][tp] = t_st
            y_last[0] = t_st
        stF["t_lastF_pe"] = t_f2mm

    seq_guard = None
    for s in range(NSEQ):
        if seq_guard is not None:
            for eng in Emitter.ENGS:
                W(eng, *seq_guard)

        W("sync", t_pro_gq)
        t_w = []
        for n in ("wq", "wk", "wv"):
            t_w.append(DMA("sync", sSY, wqkv_sb[n][:],
                           w_p[n].rearrange("a p b -> p a b")))
        EM.raw("scalar", lambda e: e.preload_act_table(AF.Sqrt))

        if s == 0:
            W("vector", t_pro_gq, (sDV, sDV.v))
        t_vset = O("vector", lambda e: e.memset(Vones[:].rearrange("p a b c -> p (a b c)"), 1.0), inc=sDV)
        st8 = dict(
            xb_free=[None, None], tmp_free=[None, None], xnT_free=[None, None],
            stage_free=[None, None], proj_psum_free=[None, None, None],
            rope_prev=[None, None], qr_free=None, kr_free=None,
            xnbf_free=None, xqbf_free=None, xnbf_spilled=None,
            xq_spill=[], t_w=t_w[-1], t_vset=t_vset,
        )
        st8["proj_psum_free"] = [t_w[0], t_w[1]]
        for i in range(NT):
            emit_E_tile(s, i, st8)
        t_E_dv = (sDV, sDV.v)
        t_E_gp = (sGP, sGP.v)
        if s == 0:
            DUMP("QT", QT, [128, 8, T], BF16, t_E_dv)
            DUMP("KT", KT, [128, 8, T], BF16, t_E_dv)
            DUMP("Von", Vones, [128, NT, H, 65], BF16, t_E_dv)

        EM.raw("scalar", lambda e: e.preload_act_table(AF.Exp))
        W("tensor", t_E_dv, t_E_gp)
        W("scalar", t_cst)
        stA = dict(
            pool_free={0: None, 1: None, 2: None},
            pt_free=[[None, None], [None, None]],
            pt_exp=[[None, None], [None, None]],
            pv_free=None, atmp_free=None, atmpbf_free=None, t_an=None,
        )
        for qb in range(4):
            for hp in range(8):
                emit_A_iter(qb, hp, stA)
        t_attnT_done = (sDV, sDV.v)
        t_A_pe = (sPE, sPE.v)
        if s == 0:
            DUMP("attnT", attnT, [128, 8, T], BF16, t_attnT_done)
        t_A_ac = (sAC, sAC.v)

        EM.raw("scalar", lambda e: e.preload_act_table(AF.Gelu))
        W("sync", t_A_pe, t_attnT_done, t_A_ac)
        t_wo = DMA("sync", sSY, wo_sb[:], w_p["wo"].rearrange("a p b -> p a b"))
        stF = dict(
            w1_used=[None, None], w2_used=[None, None],
            xqF=[None, None], xqF_used=[None, None],
            xF=[None, None], xF_used=[None, None],
            g_free=[None, None], hq_free=None, hqT_used=[None, None],
            y_ps_free=[None, None], y_sb_free=[None, None],
            h_free={0: None, 1: None, 2: None},
            stg_free=[None, None],
            t_wo=t_wo, t_attnT=t_attnT_done,
            first_gate=(sPE, sPE.v),
            t_lastF_pe=None,
        )
        stF["first_gate"] = None
        W("tensor", t_A_ac, t_attnT_done)
        W("gpsimd", t_A_pe, t_attnT_done, t_A_ac)
        for g in range(8):
            emit_F_group(s, g, stF)
            for eng in Emitter.ENGS:
                W(eng, *snap())
        seq_guard = snap()

    W("gpsimd", (sGQ, sGQ.v))
    EM.raw("gpsimd", lambda e: e.wait_ge(sGQ.h, sGQ.v))

    with nc.Block() as block:
        @block.sync
        def _(e):
            E = EngCtx(e)
            for fn in EM.ops["sync"]:
                fn(E)

        @block.tensor
        def _(e):
            E = EngCtx(e)
            for fn in EM.ops["tensor"]:
                fn(E)

        @block.vector
        def _(e):
            E = EngCtx(e)
            for fn in EM.ops["vector"]:
                fn(E)

        @block.scalar
        def _(e):
            E = EngCtx(e)
            for fn in EM.ops["scalar"]:
                fn(E)

        @block.gpsimd
        def _(e):
            E = EngCtx(e)
            for fn in EM.ops["gpsimd"]:
                fn(E)

    nc._dbg_names = list(dbg_tensors.keys())
    return nc


def _host_tables():
    pos = np.arange(T, dtype=np.float32)
    i = np.arange(HD // 2, dtype=np.float32)
    coss, sins = [], []
    for sc in (1.0, 4.0, 16.0):
        freqs = (1.0 / (10000.0 * sc) ** (2.0 * i / HD)).astype(np.float32)
        ang = pos[:, None] * freqs[None, :]
        coss.append(np.cos(ang).astype(np.float32))
        sins.append(np.sin(ang).astype(np.float32))
    cos = np.mean(np.stack(coss), 0).astype(np.float32)
    sin = np.mean(np.stack(sins), 0).astype(np.float32)
    return cos, sin


def kernel(L0_attn_x, L0_attn_Wq, L0_attn_Wk, L0_attn_Wv, L0_attn_Wo,
           L0_ffn_w1, L0_ffn_b1, L0_ffn_w2, L0_ffn_b2):
    x = np.ascontiguousarray(np.asarray(L0_attn_x, dtype=np.float32))
    cos, sin = _host_tables()
    ident = np.eye(128, dtype=np.float32).astype(ml_dtypes.bfloat16)
    shf = np.zeros((64, 128), dtype=np.float32)
    shf[np.arange(64), np.arange(64) + 64] = 1.0
    shf = shf.astype(ml_dtypes.bfloat16)
    cst = np.zeros((128, 8), dtype=np.float32)
    cst[:, 0] = EXP_BIAS
    cst[:, 1] = MAGIC

    common = {
        "wq": np.ascontiguousarray(L0_attn_Wq, dtype=np.float32),
        "wk": np.ascontiguousarray(L0_attn_Wk, dtype=np.float32),
        "wv": np.ascontiguousarray(L0_attn_Wv, dtype=np.float32),
        "wo": np.ascontiguousarray(L0_attn_Wo, dtype=np.float32),
        "w1": np.ascontiguousarray(L0_ffn_w1, dtype=np.float32),
        "b1": np.ascontiguousarray(np.asarray(L0_ffn_b1, dtype=np.float32).reshape(1, FF)),
        "w2": np.ascontiguousarray(L0_ffn_w2, dtype=np.float32),
        "b2": np.ascontiguousarray(np.asarray(L0_ffn_b2, dtype=np.float32).reshape(1, D)),
        "cos_t": cos, "sin_t": sin, "ident": ident, "shf": shf, "cst": cst,
    }
    nc = build_program()
    in_maps = []
    for c in range(NCORES):
        m = dict(common)
        m["x"] = np.ascontiguousarray(x[c * NSEQ:(c + 1) * NSEQ])
        in_maps.append(m)
    res = run_bass_kernel_spmd(nc, in_maps, core_ids=list(range(NCORES)))
    return np.concatenate([r["y"] for r in res.results], axis=0)


# version 06
# speedup vs baseline: 1.0316x; 1.0316x over previous
import numpy as np
import ml_dtypes
from contextlib import ExitStack

import concourse.bass as bass
import concourse.mybir as mybir
from concourse.bass_utils import run_bass_kernel_spmd

F32 = mybir.dt.float32
BF16 = mybir.dt.bfloat16
AF = mybir.ActivationFunctionType
ALU = mybir.AluOpType
AX = mybir.AxisListType

MAGIC = 1.5 * 2.0 ** 23
EXP_BIAS = -2.0
EPS = 1e-5
D, T, H, HD, FF = 1024, 2048, 16, 64, 4096
NT, NSEQ, NCORES = 16, 2, 8
DEBUG = False


class Sem:
    def __init__(self, nc, ctx, name):
        self.h = ctx.enter_context(nc.semaphore(name))
        self.v = 0


class Emitter:
    ENGS = ("tensor", "vector", "scalar", "gpsimd", "sync")

    def __init__(self):
        self.ops = {k: [] for k in self.ENGS}
        self.vec_sem = None
        self.sca_sem = None
        self.flush_dst_v = None
        self.flush_dst_s = None

    def wait(self, eng, *toks):
        toks = tuple(t for t in toks if t is not None)
        if not toks:
            return
        def fn(E, toks=toks):
            for t in toks:
                E.wait(t)
        self.ops[eng].append(fn)

    def op(self, eng, f, inc=None, n=1, flush_ap=None):
        if eng in ("vector", "scalar") and self.vec_sem is not None:
            sem = inc if inc is not None else self.vec_sem
            sem.v += n if inc is not None else 1
            tok = (sem, sem.v)
            def fn(E, f=f, sem=sem, v=tok[1]):
                f(E.e).then_inc(sem.h, v - (sem_prev[0] if False else 0) if False else (n if False else 1))
            def fn(E, f=f, sem=sem, nn=(n if inc is not None else 1)):
                f(E.e).then_inc(sem.h, nn)
            self.ops[eng].append(fn)
            self.wait(eng, tok)
            return tok
        if inc is None:
            self.ops[eng].append(lambda E, f=f: f(E.e))
            return None
        inc.v += n
        tok = (inc, inc.v)
        def fn(E, f=f, inc=inc, n=n):
            f(E.e).then_inc(inc.h, n)
        self.ops[eng].append(fn)
        return tok

    def dma(self, eng, sem, out, in_, **kw):
        sem.v += 16
        tok = (sem, sem.v)
        def fn(E, out=out, in_=in_, kw=kw, sem=sem):
            E.e.dma_start(out=out, in_=in_, **kw).then_inc(sem.h, 16)
        self.ops[eng].append(fn)
        return tok

    def raw(self, eng, f):
        self.ops[eng].append(lambda E, f=f: f(E.e))


class EngCtx:
    def __init__(self, e):
        self.e = e
        self.seen = {}

    def wait(self, tok):
        sem, val = tok
        if val <= 0 or self.seen.get(id(sem), -1) >= val:
            return
        self.e.wait_ge(sem.h, val)
        self.seen[id(sem)] = val


def build_program():
    nc = bass.Bass("TRN2", target_bir_lowering=False)
    ctx = ExitStack()
    nc._kernel_ctx = ctx
    dbg_tensors = {}

    x_in = nc.dram_tensor("x", [NSEQ, T, D], F32, kind="ExternalInput").ap()
    w_in = {n: nc.dram_tensor(n, [D, D], F32, kind="ExternalInput").ap()
            for n in ("wq", "wk", "wv", "wo")}
    w1_in = nc.dram_tensor("w1", [FF, D], F32, kind="ExternalInput").ap()
    b1_in = nc.dram_tensor("b1", [1, FF], F32, kind="ExternalInput").ap()
    w2_in = nc.dram_tensor("w2", [D, FF], F32, kind="ExternalInput").ap()
    b2_in = nc.dram_tensor("b2", [1, D], F32, kind="ExternalInput").ap()
    cos_in = nc.dram_tensor("cos_t", [T, 32], F32, kind="ExternalInput").ap()
    sin_in = nc.dram_tensor("sin_t", [T, 32], F32, kind="ExternalInput").ap()
    id_in = nc.dram_tensor("ident", [128, 128], BF16, kind="ExternalInput").ap()
    shf_in = nc.dram_tensor("shf", [64, 128], BF16, kind="ExternalInput").ap()
    cst_in = nc.dram_tensor("cst", [128, 8], F32, kind="ExternalInput").ap()
    y_out = nc.dram_tensor("y", [NSEQ, T, D], F32, kind="ExternalOutput").ap()
    w_p = {n: nc.dram_tensor(n + "_p", [8, 128, D], BF16, kind="Internal").ap()
           for n in ("wq", "wk", "wv", "wo")}
    w1_p = nc.dram_tensor("w1_p", [8, 128, FF], BF16, kind="Internal").ap()
    w2_p = nc.dram_tensor("w2_p", [32, 128, D], BF16, kind="Internal").ap()
    scl_p = nc.dram_tensor("scl_p", [4, 1], F32, kind="Internal").ap()
    xq_p = nc.dram_tensor("xq_p", [8, 128, T], BF16, kind="Internal").ap()

    def arena(name, nbytes):
        return ctx.enter_context(nc.sbuf_tensor(name, [128, nbytes // 4], F32))

    AR_P = arena("arP", 17408)
    AR_B = arena("arB", 98816)
    AR_C = arena("arC", 49152)
    AR_D = arena("arD", 46080)

    def carve(ar, off, shape, dtype):
        nel = int(np.prod(shape[1:]))
        assert off % 4 == 0
        if dtype == BF16:
            assert nel % 2 == 0
            v = ar[:][0:shape[0], off // 4: off // 4 + nel // 2].bitcast(BF16)
        else:
            v = ar[:][0:shape[0], off // 4: off // 4 + nel]
        if len(shape) > 2:
            pat = " ".join(f"d{i}" for i in range(1, len(shape)))
            v = v.rearrange(f"p ({pat}) -> p {pat}",
                            **{f"d{i}": shape[i] for i in range(1, len(shape))})
        return v

    cos_tok = carve(AR_P, 0, [128, NT, 32], F32)
    sin_tok = carve(AR_P, 2048, [128, NT, 32], F32)
    cstf = carve(AR_P, 4096, [128, 8], F32)
    ident = carve(AR_P, 4128, [128, 128], BF16)
    shfI = carve(AR_P, 4384, [64, 128], BF16)
    ones_bf = carve(AR_P, 4640, [4, 128], BF16)
    onesrow_f = carve(AR_P, 4896, [1, 128], F32)
    onescol_f = carve(AR_P, 5408, [128, 1], F32)
    onescol_b = carve(AR_P, 5412, [128, 2], BF16)
    b1rows = carve(AR_P, 5440, [2, FF], BF16)
    b2rows = carve(AR_P, 13632, [2, D], BF16)
    STATS = carve(AR_P, 15680, [128, 176], F32)
    _so = [0]
    def stat(n=1):
        o = _so[0]; _so[0] += n
        assert _so[0] <= 176
        return STATS[:, o:o + n]

    QT = carve(AR_B, 0, [128, 8, T], BF16)
    KT = carve(AR_B, 32768, [128, 8, T], BF16)
    Vones = carve(AR_B, 65536, [128, NT, H, 65], BF16)
    gF = [carve(AR_B, 0, [128, FF], F32), carve(AR_B, 16384, [128, FF], F32)]
    hqF = carve(AR_B, 32768, [128, FF], BF16)
    hqT = [carve(AR_B, 40960, [128, 32, 128], BF16),
           carve(AR_B, 49152, [128, 32, 128], BF16)]
    xqF = [carve(AR_B, 57344, [128, 8, 128], BF16),
           carve(AR_B, 59392, [128, 8, 128], BF16)]
    xF = [carve(AR_B, 61440, [128, D], F32), carve(AR_B, 65536, [128, D], F32)]
    yF = [carve(AR_B, 69632, [128, D], F32), carve(AR_B, 73728, [128, D], F32)]
    sqscrapF = carve(AR_B, 77824, [128, 512], F32)
    wbufP = [carve(AR_B, 0, [128, 4096], F32), carve(AR_B, 16384, [128, 4096], F32)]
    wtmpP = carve(AR_B, 32768, [128, 4096], F32)
    woutP = carve(AR_B, 49152, [128, 4096], BF16)
    toutP = carve(AR_B, 57344, [128, 32, 128], BF16)
    browP = carve(AR_B, 65536, [1, FF], F32)
    blo_bf = carve(AR_B, 84992, [1, FF], BF16)
    absaccP = carve(AR_B, 81920, [128, 256], F32)
    red16P = carve(AR_B, 83968, [128, 16], F32)

    wqkv_sb = {n: carve(AR_C, i * 16384, [128, 8, D], BF16)
               for i, n in enumerate(("wq", "wk", "wv"))}
    PTb = [[carve(AR_C, (h * 2 + p) * 2048, [128, 1024], BF16) for p in range(2)]
           for h in range(2)]
    arecip = carve(AR_C, 8192, [1, 512], F32)
    arhi = carve(AR_C, 10240, [1, 512], BF16)
    arlo = carve(AR_C, 11264, [1, 512], BF16)
    arlof = carve(AR_C, 12288, [1, 512], F32)
    atmp = carve(AR_C, 14336, [64, 512], F32)
    atmpbf = carve(AR_C, 16384, [64, 512], BF16)
    w1db = [carve(AR_C, 0, [128, 8, 512], BF16), carve(AR_C, 8192, [128, 8, 512], BF16)]
    wo_sb = carve(AR_C, 16384, [128, 8, D], BF16)
    w2db = [carve(AR_C, 32768, [128, D], BF16), carve(AR_C, 34816, [128, D], BF16)]
    xbE = [carve(AR_D, 0, [128, D], F32), carve(AR_D, 4096, [128, D], F32)]
    tmpE = [carve(AR_D, 8192, [128, D], F32), carve(AR_D, 12288, [128, D], F32)]
    xnE = carve(AR_D, 16384, [128, D], F32)
    QfE = carve(AR_D, 20480, [128, D], F32)
    KfE = carve(AR_D, 24576, [128, D], F32)
    QrE = carve(AR_D, 28672, [128, D], BF16)
    KrE = carve(AR_D, 30720, [128, D], BF16)
    xnbfE = carve(AR_D, 32768, [128, D], BF16)
    xqbfE = carve(AR_D, 34816, [128, D], BF16)
    xnTE = [carve(AR_D, 36864, [128, 8, 128], BF16),
            carve(AR_D, 38912, [128, 8, 128], BF16)]
    tA = carve(AR_D, 40960, [128, 16, 32], F32)
    tB = carve(AR_D, 43008, [128, 16, 32], F32)
    attnT = carve(AR_D, 0, [128, 8, T], BF16)

    PS = ctx.enter_context(nc.psum_tensor("ps", [128, 4096], F32))
    def psv(off, shape, dtype=F32):
        w = shape[1] if dtype == F32 else shape[1] // 2
        v = PS[:][0:shape[0], off: off + w]
        return v.bitcast(BF16) if dtype == BF16 else v
    psQ, psK, psV = psv(0, [128, 1024]), psv(1024, [128, 1024]), psv(2048, [128, 1024])
    psStg = [psv(3072, [128, 1024], BF16), psv(3584, [128, 1024], BF16)]
    psPool = [psv(0, [128, 1024]), psv(1024, [128, 1024]), psv(2048, [128, 1024])]
    psPV = [psv(3072, [65, 512]), psv(3584, [65, 512])]
    psRep = psv(0, [64, 512])
    psShift = psv(512, [128, 512])
    psH = [psv(0, [128, 512]), psv(512, [128, 512]), psv(1024, [128, 512])]
    psStgF = [psv(0, [128, 1024], BF16), psv(512, [128, 1024], BF16)]
    psY = [psv(1536, [128, 1024]), psv(2560, [128, 1024])]

    nc.const_aps.register(F32, EXP_BIAS, cstf[:, 0:1])
    nc.const_aps.register(F32, MAGIC, cstf[:, 1:2])
    nc.const_aps.register(F32, EPS, cstf[:, 2:3])

    sSY = Sem(nc, ctx, "sSY"); sGQ = Sem(nc, ctx, "sGQ")
    sPE = Sem(nc, ctx, "sPE"); sDV = Sem(nc, ctx, "sDV")
    sAC = Sem(nc, ctx, "sAC"); sGP = Sem(nc, ctx, "sGP")
    ALLSEMS = (sSY, sGQ, sPE, sDV, sAC, sGP)

    EM = Emitter()
    EM.vec_sem = sDV
    EM.sca_sem = sAC
    FLS = carve(AR_P, 16704, [128, 4], F32)
    EM.flush_dst_v = FLS[:, 0:1]
    EM.flush_dst_s = FLS[:, 1:2]
    W, O, DMA = EM.wait, EM.op, EM.dma

    def DUMP(name, ap, shape, dtype, *toks):
        if not DEBUG or name in dbg_tensors:
            return
        dt_ = nc.dram_tensor("dbg_" + name, list(shape), dtype, kind="ExternalOutput").ap()
        dbg_tensors[name] = dt_
        W("gpsimd", *toks)
        tok = DMA("gpsimd", sGQ, dt_, ap)
        for eng in Emitter.ENGS:
            W(eng, tok)

    def snap():
        return [(s, s.v) for s in ALLSEMS]

    DMA("sync", sSY, cos_tok, cos_in.rearrange("(i p) c -> p i c", p=128))
    DMA("sync", sSY, sin_tok, sin_in.rearrange("(i p) c -> p i c", p=128))
    DMA("sync", sSY, cstf, cst_in)
    DMA("sync", sSY, ident, id_in)
    DMA("sync", sSY, shfI, shf_in)
    t_b1 = DMA("sync", sSY, browP, b1_in)
    t_cst = (sSY, sSY.v)

    W("vector", t_cst)
    O("vector", lambda e: e.memset(ones_bf[:], 1.0))
    O("vector", lambda e: e.memset(onesrow_f[:], 1.0))
    O("vector", lambda e: e.memset(onescol_f[:], 1.0))
    O("vector", lambda e: e.memset(onescol_b[:, 0:1], 1.0))

    W("vector", t_b1)
    O("vector", lambda e: e.tensor_copy(b1rows[0:1, :], browP[:]))
    O("vector", lambda e: e.tensor_sub(browP[:], browP[:], b1rows[0:1, :]))
    t_b1l = O("vector", lambda e: e.tensor_copy(blo_bf[:], browP[:]), inc=sDV, flush_ap=blo_bf[0:1, 0:1])
    W("gpsimd", t_b1l)
    t_b1r = DMA("gpsimd", sGQ, b1rows[1:2, :], blo_bf[:])
    W("sync", t_b1l)
    t_b2 = DMA("sync", sSY, browP[0:1, 0:D], b2_in)
    W("vector", t_b2, t_b1r)
    O("vector", lambda e: e.tensor_copy(b2rows[0:1, :], browP[0:1, 0:D]))
    O("vector", lambda e: e.tensor_sub(browP[0:1, 0:D], browP[0:1, 0:D], b2rows[0:1, :]))
    t_b2l = O("vector", lambda e: e.tensor_copy(blo_bf[0:1, 0:D], browP[0:1, 0:D]), inc=sDV, flush_ap=blo_bf[0:1, 0:1])
    W("gpsimd", t_b2l)
    t_b2r = DMA("gpsimd", sGQ, b2rows[1:2, :], blo_bf[0:1, 0:D])

    last_wout = [None, None]
    wbuf_cons = [None, None]
    for mi, name in enumerate(("wq", "wk", "wv", "wo")):
        for dc in range(8):
            par = (mi * 8 + dc) % 2
            dsthalf = woutP[:, par * 2048 + 0: par * 2048 + D]
            W("sync", wbuf_cons[par])
            tl = DMA("sync", sSY, wbufP[par][:, 0:D],
                     w_in[name][dc * 128:(dc + 1) * 128, :])
            W("vector", tl, last_wout[par])
            tc = O("vector", lambda e, par=par, dsthalf=dsthalf:
                   e.tensor_copy(dsthalf, wbufP[par][:, 0:D]), inc=sDV)
            wbuf_cons[par] = tc
            W("gpsimd", tc)
            last_wout[par] = DMA("gpsimd", sGQ, w_p[name][dc], dsthalf)

    prevl = [None, None]
    for fc in range(32):
        par = fc % 2
        W("sync", wbuf_cons[par])
        tl = DMA("sync", sSY, wbufP[par][:, 0:D], w1_in[fc * 128:(fc + 1) * 128, :])
        W("vector", tl, prevl[par], last_wout[par])
        last_wout[par] = None
        prevl[par] = O("vector", lambda e, fc=fc, par=par: e.tensor_reduce(
            absaccP[:, fc * 8:(fc + 1) * 8].unsqueeze(2),
            wbufP[par][:, 0:D].rearrange("p (a b) -> p a b", a=8), AX.X, ALU.add,
            apply_absolute_value=True), inc=sDV)
        wbuf_cons[par] = prevl[par]
    sum1 = stat(); sum_lo1 = stat()
    O("vector", lambda e: e.tensor_reduce(red16P[:].unsqueeze(2),
                                          absaccP[:].rearrange("p (a b) -> p a b", a=16),
                                          AX.X, ALU.add))
    t_s1 = O("vector", lambda e: e.tensor_reduce(sum1, red16P[:], AX.X, ALU.add), inc=sDV)
    DUMP("absacc1", absaccP[:], [128, 256], F32, t_s1)
    for dc in range(8):
        par = dc % 2
        W("sync", wbuf_cons[par])
        tl = DMA("sync", sSY, wbufP[par][:], w2_in[dc * 128:(dc + 1) * 128, :])
        W("vector", tl, prevl[par])
        prevl[par] = O("vector", lambda e, dc=dc, par=par: e.tensor_reduce(
            absaccP[:, dc * 8:(dc + 1) * 8].unsqueeze(2),
            wbufP[par][:].rearrange("p (a b) -> p a b", a=8), AX.X, ALU.add,
            apply_absolute_value=True), inc=sDV)
        wbuf_cons[par] = prevl[par]
    sum2 = stat()
    O("vector", lambda e: e.tensor_reduce(red16P[:, 0:8].unsqueeze(2),
                                          absaccP[:, 0:64].rearrange("p (a b) -> p a b", a=8),
                                          AX.X, ALU.add))
    t_s2 = O("vector", lambda e: e.tensor_reduce(sum2, red16P[:, 0:8], AX.X, ALU.add), inc=sDV)
    DUMP("absacc2", absaccP[:], [128, 256], F32, t_s2)
    DUMP("wbuf_w2last", wbufP[1][:], [128, 4096], F32, t_s2)

    s1hi = carve(AR_P, 16384, [128, 4], BF16)
    O("vector", lambda e: e.tensor_copy(s1hi[:, 0:1], sum1))
    O("vector", lambda e: e.tensor_sub(sum_lo1, sum1, s1hi[:, 0:1]))
    O("vector", lambda e: e.tensor_copy(s1hi[:, 1:2], sum_lo1))
    O("vector", lambda e: e.tensor_copy(s1hi[:, 2:3], sum2))
    O("vector", lambda e: e.tensor_sub(sum_lo1, sum2, s1hi[:, 2:3]))
    t_hilo = O("vector", lambda e: e.tensor_copy(s1hi[:, 3:4], sum_lo1), inc=sDV)
    W("tensor", t_s1, t_s2, t_hilo)
    O("tensor", lambda e: e.matmul(PS[0:1, 0:1], s1hi[:, 0:1], onescol_b[:, 0:1], start=True, stop=False))
    O("tensor", lambda e: e.matmul(PS[0:1, 0:1], s1hi[:, 1:2], onescol_b[:, 0:1], start=False, stop=True))
    O("tensor", lambda e: e.matmul(PS[0:1, 512:513], s1hi[:, 2:3], onescol_b[:, 0:1], start=True, stop=False))
    t_m = O("tensor", lambda e: e.matmul(PS[0:1, 512:513], s1hi[:, 3:4], onescol_b[:, 0:1], start=False, stop=True), inc=sPE)
    dqw1v, dqw2v, sw1v, sw2v = stat(), stat(), stat(), stat()
    W("vector", t_m)
    O("vector", lambda e: e.tensor_scalar(dqw1v[0:1, :], PS[0:1, 0:1], 1.0 / (FF * D), 1e-5, ALU.mult, ALU.max))
    O("vector", lambda e: e.tensor_scalar(dqw2v[0:1, :], PS[0:1, 512:513], 1.0 / (FF * D), 1e-5, ALU.mult, ALU.max))
    O("vector", lambda e: e.reciprocal(sw1v[0:1, :], dqw1v[0:1, :]))
    O("vector", lambda e: e.reciprocal(sw2v[0:1, :], dqw2v[0:1, :]))
    nr1, nr2 = stat(), stat()
    for (sv, dv_) in ((sw1v, dqw1v), (sw2v, dqw2v)):
        O("vector", lambda e, sv=sv, dv_=dv_: e.tensor_mul(nr1[0:1, :], dv_[0:1, :], sv[0:1, :]))
        O("vector", lambda e: e.tensor_scalar(nr1[0:1, :], nr1[0:1, :], -1.0, 2.0, ALU.mult, ALU.add))
        O("vector", lambda e, sv=sv: e.tensor_mul(sv[0:1, :], sv[0:1, :], nr1[0:1, :]))
    t_sc = O("vector", lambda e: e.tensor_copy(nr2[0:1, :], nr1[0:1, :]), inc=sDV)
    dqw1, dqw2, sw1, sw2 = stat(), stat(), stat(), stat()
    W("gpsimd", t_sc)
    t_up = None
    for idx, srcv in enumerate((dqw1v, dqw2v, sw1v, sw2v)):
        t_up = DMA("gpsimd", sGQ, scl_p[idx], srcv[0:1, :])
    W("gpsimd", t_up)
    for idx, dstv in enumerate((dqw1, dqw2, sw1, sw2)):
        t_screp = DMA("gpsimd", sGQ, dstv, scl_p[idx].to_broadcast((128, 1)))
    DUMP("prostats", STATS[:, 0:16], [128, 16], F32, t_screp)

    def quant_chunk(par, width, sw_ap, dq_ap):
        O("vector", lambda e, par=par, width=width, sw_ap=sw_ap: e.tensor_scalar(
            wtmpP[:, 0:width], wbufP[par][:, 0:width], sw_ap, MAGIC, ALU.mult, ALU.add))
        O("vector", lambda e, width=width: e.tensor_scalar(
            wtmpP[:, 0:width], wtmpP[:, 0:width], -MAGIC, -1.0, ALU.add, ALU.max))
        return O("vector", lambda e, width=width, dq_ap=dq_ap: e.tensor_scalar(
            woutP[:, 0:width], wtmpP[:, 0:width], 1.0, dq_ap, ALU.min, ALU.mult), inc=sDV, flush_ap=woutP[:, 0:1])

    prevl = [None, None]
    stage_cons = [None, None]
    prev_quant = None
    prev_tout_dma = None
    for fc in range(32):
        par = fc % 2
        W("sync", wbuf_cons[par])
        tl = DMA("sync", sSY, wbufP[par][:, 0:D], w1_in[fc * 128:(fc + 1) * 128, :])
        W("vector", tl, t_screp, prevl[par], prev_quant)
        tq = quant_chunk(par, D, sw1, dqw1)
        if fc == 0:
            DUMP("w1q0", woutP[:, 0:D], [128, D], BF16, tq)
        prevl[par] = tq
        wbuf_cons[par] = tq
        prev_quant = None
        sp = fc % 2
        W("tensor", tq, stage_cons[sp])
        for dc in range(8):
            last = (dc == 7)
            tok = O("tensor", lambda e, dc=dc, sp=sp: e.transpose(
                psStg[sp][:, dc * 128:(dc + 1) * 128],
                woutP[:, dc * 128:(dc + 1) * 128], ident),
                inc=sPE if last else None)
        t_tr = tok
        W("vector", t_tr, prev_tout_dma)
        t_ev = O("vector", lambda e, sp=sp: e.tensor_copy(
            toutP[:, 0:8, :], psStg[sp][:].rearrange("p (a b) -> p a b", a=8)), inc=sDV, flush_ap=toutP[:, 0:1, 0:1])
        stage_cons[sp] = t_ev
        prev_quant = t_tr
        W("gpsimd", t_ev)
        prev_tout_dma = DMA("gpsimd", sGQ, w1_p[:, :, fc * 128:(fc + 1) * 128].rearrange("a p b -> p a b"),
                            toutP[:, 0:8, :])
    for dc in range(8):
        par = dc % 2
        W("sync", wbuf_cons[par])
        tl = DMA("sync", sSY, wbufP[par][:], w2_in[dc * 128:(dc + 1) * 128, :])
        W("vector", tl, prevl[par], prev_quant)
        tq = quant_chunk(par, FF, sw2, dqw2)
        prevl[par] = tq
        wbuf_cons[par] = tq
        t_ev = None
        for r in range(4):
            sp = r % 2
            W("tensor", tq, stage_cons[sp])
            for b in range(8):
                fcb = r * 8 + b
                last = (b == 7)
                tok = O("tensor", lambda e, fcb=fcb, b=b, sp=sp: e.transpose(
                    psStg[sp][:, b * 128:(b + 1) * 128],
                    woutP[:, fcb * 128:(fcb + 1) * 128], ident),
                    inc=sPE if last else None)
            W("vector", tok, prev_tout_dma if r == 0 else None)
            t_ev = O("vector", lambda e, sp=sp, r=r: e.tensor_copy(
                toutP[:, r * 8:(r + 1) * 8, :],
                psStg[sp][:].rearrange("p (a b) -> p a b", a=8)), inc=sDV)
            stage_cons[sp] = t_ev
        prev_quant = tok
        W("gpsimd", t_ev)
        prev_tout_dma = DMA("gpsimd", sGQ, w2_p[:, :, dc * 128:(dc + 1) * 128].rearrange("a p b -> p a b"),
                            toutP[:])
    t_pro_gq = (sGQ, sGQ.v)

    cE = {k: [stat(), stat()] for k in
          ("mean", "var", "rstd", "c2", "nm", "amax", "f1", "dq1", "t0", "t1", "sq")}
    cF = {k: stat() for k in ("ms", "a", "s", "rc", "am2", "d2", "f2", "dq2", "u0", "u1")}
    cF["sqc"] = [stat(8), stat(8)]

    y_last = [None]

    def emit_E_tile(s, i, st8):
        par = i % 2
        W("gpsimd", st8["xb_free"][par])
        t_x = DMA("gpsimd", sGQ, xbE[par], x_in[s, i * 128:(i + 1) * 128, :])

        W("vector", t_x)
        O("vector", lambda e, par=par: e.tensor_reduce(
            cE["t0"][par], xbE[par][:], AX.X, ALU.add))
        W("scalar", t_x, st8["tmp_free"][par])
        t_sq = O("scalar", lambda e, par=par: e.activation(
            tmpE[par][:], xbE[par][:], AF.Square, accum_out=cE["sq"][par]), inc=sAC,
            flush_ap=cE["sq"][par])
        W("vector", t_sq)
        O("vector", lambda e, par=par: e.tensor_scalar(cE["mean"][par], cE["t0"][par], 1.0 / D, None, ALU.mult))
        O("vector", lambda e, par=par: e.tensor_scalar(cE["nm"][par], cE["t0"][par], -1.0 / D, None, ALU.mult))
        O("vector", lambda e, par=par: e.tensor_mul(cE["t0"][par], cE["mean"][par], cE["mean"][par]))
        t_va = O("vector", lambda e, par=par: e.scalar_tensor_tensor(
            cE["var"][par], cE["sq"][par], 1.0 / D, cE["t0"][par], ALU.mult, ALU.subtract), inc=sDV)
        W("scalar", t_va)
        t_sq1 = O("scalar", lambda e, par=par: e.activation(cE["t1"][par], cE["var"][par], AF.Sqrt, bias=EPS), inc=sAC,
            flush_ap=cE["t1"][par])
        W("vector", t_sq1)
        O("vector", lambda e, par=par: e.reciprocal(cE["rstd"][par], cE["t1"][par]))
        O("vector", lambda e, par=par: e.tensor_mul(cE["t0"][par], cE["rstd"][par], cE["rstd"][par]))
        t_vb = O("vector", lambda e, par=par: e.tensor_mul(
            cE["t0"][par], cE["var"][par], cE["t0"][par]), inc=sDV)
        W("scalar", t_vb)
        t_sq2 = O("scalar", lambda e, par=par: e.activation(cE["t1"][par], cE["t0"][par], AF.Sqrt, bias=EPS), inc=sAC,
            flush_ap=cE["t1"][par])
        W("vector", t_sq2)
        O("vector", lambda e, par=par: e.reciprocal(cE["c2"][par], cE["t1"][par]))
        t_xn = O("vector", lambda e, par=par: e.tensor_scalar(
            xnE[:], xbE[par][:], cE["nm"][par], cE["rstd"][par], ALU.add, ALU.mult), inc=sDV)
        st8["xb_free"][par] = t_xn
        W("vector", st8["xnbf_free"])
        t_xnbf = O("vector", lambda e: e.tensor_copy(xnbfE[:], xnE[:]), inc=sDV, flush_ap=xnbfE[:, 0:1])
        O("vector", lambda e, par=par: e.tensor_reduce(
            cE["amax"][par], xnE[:], AX.X, ALU.max, apply_absolute_value=True))
        O("vector", lambda e, par=par: e.tensor_scalar(
            cE["t0"][par], cE["c2"][par], cE["amax"][par], 1e-5, ALU.mult, ALU.max))
        O("vector", lambda e, par=par: e.tensor_scalar(cE["dq1"][par], cE["t0"][par], 1.0 / 127.0, None, ALU.mult))
        O("vector", lambda e, par=par: e.reciprocal(cE["t1"][par], cE["t0"][par]))
        O("vector", lambda e, par=par: e.tensor_scalar(
            cE["f1"][par], cE["t1"][par], cE["c2"][par], 127.0, ALU.mult, ALU.mult))
        O("vector", lambda e, par=par: e.tensor_scalar(
            tmpE[par][:], xnE[:], cE["f1"][par], MAGIC, ALU.mult, ALU.add))
        W("vector", st8["xqbf_free"])
        t_xq = O("vector", lambda e, par=par: e.tensor_scalar(
            xqbfE[:], tmpE[par][:], -MAGIC, cE["dq1"][par], ALU.add, ALU.mult), inc=sDV, flush_ap=xqbfE[:, 0:1])
        st8["tmp_free"][par] = t_xq
        if s == 0 and i == 0:
            DUMP("xn", xnE[:], [128, D], F32, t_xn)
            DUMP("xq", xqbfE[:], [128, D], BF16, t_xq)
            DUMP("estats", STATS[:, 16:48], [128, 32], F32, t_xq)

        W("tensor", t_xnbf, st8["stage_free"][0], st8["xnT_free"][par])
        for dc in range(8):
            tok = O("tensor", lambda e, dc=dc: e.transpose(
                psStg[0][:, dc * 128:(dc + 1) * 128], xnbfE[:, dc * 128:(dc + 1) * 128], ident),
                inc=sPE if dc == 7 else None)
        t_trxn = tok
        W("tensor", t_xq, st8["stage_free"][1])
        for dc in range(8):
            tok = O("tensor", lambda e, dc=dc: e.transpose(
                psStg[1][:, dc * 128:(dc + 1) * 128], xqbfE[:, dc * 128:(dc + 1) * 128], ident),
                inc=sPE if dc == 7 else None)
        t_trxq = tok
        st8["xqbf_free"] = t_trxq
        W("vector", t_trxn)
        t_xnT = O("vector", lambda e, par=par: e.tensor_copy(
            xnTE[par][:], psStg[0][:].rearrange("p (a b) -> p a b", a=8)), inc=sDV, flush_ap=xnTE[par][:, 0:1, 0:1])
        st8["stage_free"][0] = t_xnT
        W("vector", t_trxq, st8["xnbf_spilled"])
        t_xqT = O("vector", lambda e: e.tensor_copy(
            xnbfE[:].rearrange("p (a b) -> p a b", a=8),
            psStg[1][:].rearrange("p (a b) -> p a b", a=8)), inc=sDV, flush_ap=xnbfE[:, 0:1])
        st8["stage_free"][1] = t_xqT
        st8["xnbf_free"] = None
        W("gpsimd", t_xqT)
        t_sp = DMA("gpsimd", sGQ, xq_p[:, :, i * 128:(i + 1) * 128].rearrange("a p b -> p a b"),
                   xnbfE[:].rearrange("p (a b) -> p a b", a=8))
        st8["xq_spill"].append(t_sp)
        st8["xnbf_spilled"] = t_sp
        st8["xnbf_free"] = t_sp

        W("tensor", t_xnT, st8["t_w"], *st8["proj_psum_free"])
        for dc in range(8):
            st_, sp_ = (dc == 0), (dc == 7)
            for k, (ps_, nm) in enumerate(((psQ, "wq"), (psK, "wk"), (psV, "wv"))):
                for half in range(2):
                    last = sp_ and k == 2 and half == 1
                    tok = O("tensor", lambda e, dc=dc, ps_=ps_, nm=nm, half=half, st_=st_, sp_=sp_, par=par:
                            e.matmul(ps_[:, half * 512:(half + 1) * 512],
                                     xnTE[par][:, dc, :],
                                     wqkv_sb[nm][:, dc, half * 512:(half + 1) * 512],
                                     start=st_, stop=sp_),
                            inc=sPE if last else None)
        t_proj = tok
        st8["xnT_free"][par] = t_proj

        W("scalar", t_proj, *st8["rope_prev"])
        t_qf = O("scalar", lambda e: e.copy(QfE[:], psQ[:]), inc=sAC, flush_ap=QfE[:, 0:1])
        t_kf = O("scalar", lambda e: e.copy(KfE[:], psK[:]), inc=sAC, flush_ap=KfE[:, 0:1])
        W("vector", t_proj, st8["t_vset"])
        t_vev = O("vector", lambda e, i=i: e.tensor_copy(
            Vones[:, i, :, 0:64], psV[:].rearrange("p (h c) -> p h c", c=64)), inc=sDV, flush_ap=Vones[:, 0:1, 0:1, 0:1])
        st8["proj_psum_free"] = [t_qf, t_kf, t_vev]

        cosb = cos_tok[:, i, :].unsqueeze(1).to_broadcast((128, 16, 32))
        sinb = sin_tok[:, i, :].unsqueeze(1).to_broadcast((128, 16, 32))
        def rv(buf, parity):
            return buf.rearrange("p (h q t) -> p h q t", h=16, q=32)[:, :, :, parity]
        QE_, QO_ = rv(QfE[:], 0), rv(QfE[:], 1)
        KE_, KO_ = rv(KfE[:], 0), rv(KfE[:], 1)
        QrEv, QrOv = rv(QrE[:], 0), rv(QrE[:], 1)
        KrEv, KrOv = rv(KrE[:], 0), rv(KrE[:], 1)
        tAv, tBv = tA, tB
        W("vector", t_qf, st8["qr_free"])
        O("vector", lambda e, QE_=QE_, cosb=cosb, tAv=tAv: e.tensor_tensor(tAv, QE_, cosb, ALU.mult))
        O("vector", lambda e, QO_=QO_, sinb=sinb, tBv=tBv: e.tensor_tensor(tBv, QO_, sinb, ALU.mult))
        O("vector", lambda e, QrEv=QrEv, tAv=tAv, tBv=tBv: e.tensor_tensor(QrEv, tAv, tBv, ALU.subtract))
        O("vector", lambda e, QE_=QE_, sinb=sinb, tAv=tAv: e.tensor_tensor(tAv, QE_, sinb, ALU.mult))
        O("vector", lambda e, QO_=QO_, cosb=cosb, tBv=tBv: e.tensor_tensor(tBv, QO_, cosb, ALU.mult))
        t_ropeq = O("vector", lambda e, QrOv=QrOv, tAv=tAv, tBv=tBv: e.tensor_tensor(QrOv, tAv, tBv, ALU.add), inc=sDV, flush_ap=QrE[:, 0:1])
        W("gpsimd", t_kf, st8["kr_free"])
        O("gpsimd", lambda e, KrEv=KrEv, KE_=KE_, cosb=cosb: e.tensor_tensor(KrEv, KE_, cosb, ALU.mult))
        O("gpsimd", lambda e, KrOv=KrOv, KO_=KO_, sinb=sinb: e.tensor_tensor(KrOv, KO_, sinb, ALU.mult))
        O("gpsimd", lambda e, KrEv=KrEv, KrOv=KrOv: e.tensor_tensor(KrEv, KrEv, KrOv, ALU.subtract))
        O("gpsimd", lambda e, KrOv=KrOv, KO_=KO_, cosb=cosb: e.tensor_tensor(KrOv, KO_, cosb, ALU.mult))
        O("gpsimd", lambda e, KE_=KE_, sinb=sinb: e.tensor_tensor(KE_, KE_, sinb, ALU.mult))
        t_ropek = O("gpsimd", lambda e, KrOv=KrOv, KE_=KE_: e.tensor_tensor(KrOv, KrOv, KE_, ALU.add), inc=sGP)
        st8["rope_prev"] = [t_ropeq, t_ropek]
        if s == 0 and i == 0:
            DUMP("qf", QfE[:], [128, D], F32, t_qf)
            DUMP("qr", QrE[:], [128, D], BF16, t_ropeq)
            DUMP("kr", KrE[:], [128, D], BF16, t_ropek)

        W("tensor", t_ropeq, st8["stage_free"][0])
        for dc in range(8):
            tok = O("tensor", lambda e, dc=dc: e.transpose(
                psStg[0][:, dc * 128:(dc + 1) * 128], QrE[:, dc * 128:(dc + 1) * 128], ident),
                inc=sPE if dc == 7 else None)
        t_trq = tok
        st8["qr_free"] = t_trq
        W("tensor", t_ropek, st8["stage_free"][1])
        for dc in range(8):
            tok = O("tensor", lambda e, dc=dc: e.transpose(
                psStg[1][:, dc * 128:(dc + 1) * 128], KrE[:, dc * 128:(dc + 1) * 128], ident),
                inc=sPE if dc == 7 else None)
        t_trk = tok
        st8["kr_free"] = t_trk
        W("vector", t_trq)
        t_qtw = O("vector", lambda e, i=i: e.tensor_copy(
            QT[:, :, i * 128:(i + 1) * 128], psStg[0][:].rearrange("p (a b) -> p a b", a=8)), inc=sDV, flush_ap=QT[:, 0:1, 0:1])
        st8["stage_free"][0] = t_qtw
        W("vector", t_trk)
        t_ktw = O("vector", lambda e, i=i: e.tensor_copy(
            KT[:, :, i * 128:(i + 1) * 128], psStg[1][:].rearrange("p (a b) -> p a b", a=8)), inc=sDV, flush_ap=KT[:, 0:1, 0:1])
        st8["stage_free"][1] = t_ktw

    def emit_A_iter(qb, hp, stA):
        t_pv_ = None
        for j in range(8):
            p0, p1, pp = (2 * j) % 3, (2 * j + 1) % 3, j % 2
            W("tensor", stA["pool_free"][p0], stA["pool_free"][p1])
            for kc2 in range(2):
                kc = 2 * j + kc2
                for h in range(2):
                    pool = psPool[p0 if h == 0 else p1]
                    lo = 0 if h == 0 else 64
                    last = (kc2 == 1 and h == 1)
                    tok = O("tensor", lambda e, kc=kc, kc2=kc2, lo=lo, pool=pool, hp=hp, qb=qb:
                            e.matmul(pool[:, kc2 * 512:(kc2 + 1) * 512],
                                     KT[lo:lo + 64, hp, kc * 128:(kc + 1) * 128],
                                     QT[lo:lo + 64, hp, qb * 512:(qb + 1) * 512],
                                     start=True, stop=True, tile_position=(lo, 0)),
                            inc=sPE if last else None)
            t_sc_ = tok
            for h, pi in ((0, p0), (1, p1)):
                W("scalar", t_sc_, stA["pt_free"][h][pp])
                t_e = O("scalar", lambda e, h=h, pi=pi, pp=pp: e.activation(
                    PTb[h][pp][:], psPool[pi][:], AF.Exp, bias=EXP_BIAS, scale=0.125), inc=sAC,
                    flush_ap=PTb[h][pp][:, 0:1])
                stA["pool_free"][pi] = t_e
                stA["pt_exp"][h][pp] = t_e
                if h == 0 and j == 0:
                    DUMP("pt00", PTb[0][0][:], [128, 1024], BF16, t_e)
            st_ = (j == 0)
            W("tensor", stA["pt_exp"][0][pp], stA["pt_exp"][1][pp],
              stA["pv_free"] if st_ else None)
            for h in range(2):
                for kc2 in range(2):
                    kc = 2 * j + kc2
                    h_abs = 2 * hp + h
                    last = (h == 1 and kc2 == 1)
                    tok = O("tensor", lambda e, h=h, kc=kc, kc2=kc2, h_abs=h_abs, pp=pp, st_=st_:
                            e.matmul(psPV[h][:],
                                     Vones[:, kc, h_abs, :],
                                     PTb[h][pp][:, kc2 * 512:(kc2 + 1) * 512],
                                     start=(st_ and kc2 == 0), stop=(kc == 15 and kc2 == 1)),
                            inc=sPE if last else None)
            t_pv_ = tok
            stA["pt_free"][0][pp] = t_pv_
            stA["pt_free"][1][pp] = t_pv_
        t_an = None
        for h in range(2):
            W("vector", t_pv_, stA["atmp_free"])
            O("vector", lambda e, h=h: e.reciprocal(arecip[:], psPV[h][64:65, :]))
            O("vector", lambda e: e.tensor_copy(arhi[:], arecip[:]), flush_ap=arhi[0:1, 0:1])
            O("vector", lambda e: e.tensor_sub(arlof[:], arecip[:], arhi[:]))
            t_hl = O("vector", lambda e: e.tensor_copy(arlo[:], arlof[:]), inc=sDV, flush_ap=arlo[0:1, 0:1])
            W("tensor", t_hl, stA["pool_free"][0])
            O("tensor", lambda e: e.matmul(psRep[:], ones_bf[0:1, 0:64], arhi[:], start=True, stop=False))
            t_rp = O("tensor", lambda e: e.matmul(psRep[:], ones_bf[0:1, 0:64], arlo[:], start=False, stop=True), inc=sPE)
            W("scalar", t_pv_, stA["atmp_free"])
            t_at = O("scalar", lambda e, h=h: e.copy(atmp[:], psPV[h][0:64, :]), inc=sAC, flush_ap=atmp[:, 0:1])
            if h == 0:
                DUMP("recip00", arecip[:], [1, 512], F32, t_hl)
                DUMP("pv00", atmp[:], [64, 512], F32, t_at)
            if h == 0:
                W("vector", t_at, t_rp)
                t_an = O("vector", lambda e, hp=hp, qb=qb: e.tensor_mul(
                    attnT[0:64, hp, qb * 512:(qb + 1) * 512], atmp[:], psRep[:]), inc=sDV, flush_ap=attnT[0:64, hp, qb * 512:qb * 512 + 1])
            else:
                W("vector", t_at, t_rp, stA["atmpbf_free"])
                t_nb = O("vector", lambda e: e.tensor_mul(atmpbf[:], atmp[:], psRep[:]), inc=sDV, flush_ap=atmpbf[:, 0:1])
                W("tensor", t_nb, stA["pool_free"][0])
                t_sh = O("tensor", lambda e: e.matmul(
                    psShift[:], shfI[:], atmpbf[:], start=True, stop=True), inc=sPE)
                W("vector", t_sh)
                t_an = O("vector", lambda e, hp=hp, qb=qb: e.tensor_copy(
                    attnT[64:128, hp, qb * 512:(qb + 1) * 512], psShift[64:128, :]), inc=sDV, flush_ap=attnT[64:128, hp, qb * 512:qb * 512 + 1])
                stA["atmpbf_free"] = t_an
            stA["pool_free"][0] = t_an
            stA["atmp_free"] = t_an
        stA["pv_free"] = t_an
        stA["t_an"] = t_an

    def emit_F_group(s, g, stF):
        tiles = [2 * g, 2 * g + 1]
        for tp, i in enumerate(tiles):
            W("gpsimd", stF["xqF_used"][tp])
            stF["xqF"][tp] = DMA("gpsimd", sGQ, xqF[tp][:],
                                 xq_p[:, :, i * 128:(i + 1) * 128].rearrange("a p b -> p a b"))
            W("gpsimd", stF["xF_used"][tp])
            stF["xF"][tp] = DMA("gpsimd", sGQ, xF[tp], x_in[s, i * 128:(i + 1) * 128, :])
        t_gelu_last = [None, None]
        for fb in range(8):
            wpar = fb % 2
            W("sync", stF["w1_used"][wpar])
            t_w1 = DMA("sync", sSY, w1db[wpar][:],
                       w1_p[:, :, fb * 512:(fb + 1) * 512].rearrange("a p b -> p a b"))
            for tp in range(2):
                hb = (fb * 2 + tp) % 3
                W("tensor", t_w1, stF["xqF"][tp], stF["h_free"][hb],
                  stF["first_gate"])
                stF["first_gate"] = None
                for dc in range(8):
                    O("tensor", lambda e, dc=dc, tp=tp, wpar=wpar, hb=hb: e.matmul(
                        psH[hb][:], xqF[tp][:, dc, :], w1db[wpar][:, dc, :],
                        start=(dc == 0), stop=False))
                t_h = O("tensor", lambda e, hb=hb, fb=fb: e.matmul(
                    psH[hb][:], ones_bf[0:2, :], b1rows[:, fb * 512:(fb + 1) * 512],
                    start=False, stop=True), inc=sPE)
                stF["w1_used"][wpar] = t_h
                if s == 0 and g == 0 and fb == 3 and tp == 0:
                    DUMP("w1db3", w1db[wpar][:], [128, 8, 512], BF16, t_w1)
                    DUMP("xqF0", xqF[0][:], [128, 8, 128], BF16, stF["xqF"][0])
                    DUMP("b1r", b1rows[:], [2, FF], BF16, t_h)
                if s == 0 and g == 0 and fb == 2 and tp == 0:
                    DUMP("w1db2", w1db[wpar][:], [128, 8, 512], BF16, t_w1)
                    DUMP("g2early", gF[0][:, 1024:1536], [128, 512], F32, t_gl)
                W("scalar", t_h, stF["g_free"][tp] if fb == 0 else None)
                t_gl = O("scalar", lambda e, tp=tp, fb=fb, hb=hb: e.activation(
                    gF[tp][:, fb * 512:(fb + 1) * 512], psH[hb][:], AF.Gelu), inc=sAC)
                stF["h_free"][hb] = t_gl
                t_sq_ = O("scalar", lambda e, tp=tp, fb=fb: e.activation(
                    sqscrapF[:], gF[tp][:, fb * 512:(fb + 1) * 512], AF.Square,
                    accum_out=cF["sqc"][tp][:, fb:fb + 1]), inc=sAC,
                    flush_ap=cF["sqc"][tp][:, fb:fb + 1])
                t_gelu_last[tp] = t_sq_
        t_hqT_ready = [None, None]
        EM.raw("scalar", lambda e: e.preload_act_table(AF.Sqrt))
        for tp in range(2):
            i = tiles[tp]
            W("vector", t_gelu_last[tp])
            O("vector", lambda e, tp=tp: e.tensor_reduce(
                cF["am2"], gF[tp][:], AX.X, ALU.max, apply_absolute_value=True))
            O("vector", lambda e, tp=tp: e.tensor_reduce(cF["ms"], cF["sqc"][tp][:], AX.X, ALU.add))
            t_a = O("vector", lambda e: e.tensor_scalar(cF["a"], cF["ms"], 1.0 / FF, None, ALU.mult), inc=sDV)
            W("scalar", t_a)
            t_sqf = O("scalar", lambda e: e.activation(cF["s"], cF["a"], AF.Sqrt, bias=EPS), inc=sAC,
                      flush_ap=cF["s"])
            W("vector", t_sqf)
            O("vector", lambda e: e.reciprocal(cF["rc"], cF["s"]))
            O("vector", lambda e: e.tensor_scalar(cF["d2"], cF["rc"], cF["am2"], 1e-5, ALU.mult, ALU.max))
            O("vector", lambda e: e.tensor_scalar(cF["dq2"], cF["d2"], 1.0 / 127.0, None, ALU.mult))
            O("vector", lambda e: e.reciprocal(cF["u0"], cF["d2"]))
            t_f2 = O("vector", lambda e: e.tensor_scalar(cF["f2"], cF["u0"], cF["rc"], 127.0, ALU.mult, ALU.mult), inc=sDV)
            if s == 0 and g == 0 and tp == 0:
                DUMP("g0", gF[0][:], [128, FF], F32, t_gelu_last[0])
                DUMP("fstats", STATS[:, 32:64], [128, 32], F32, t_f2)
            W("scalar", t_f2)
            t_p1 = O("scalar", lambda e, tp=tp: e.activation(
                gF[tp][:], gF[tp][:], AF.Copy, bias=MAGIC, scale=cF["f2"]), inc=sAC, flush_ap=gF[tp][:, 0:1])
            W("vector", t_p1, stF["hq_free"])
            t_hq = O("vector", lambda e, tp=tp: e.tensor_scalar(
                hqF[:], gF[tp][:], -MAGIC, cF["dq2"], ALU.add, ALU.mult), inc=sDV, flush_ap=hqF[:, 0:1])
            stF["g_free"][tp] = t_hq
            if s == 0 and g == 0 and tp == 0:
                DUMP("hq0", hqF[:], [128, FF], BF16, t_hq)
            if s == 1 and g == 5 and tp == 0:
                DUMP("hqX", hqF[:], [128, FF], BF16, t_hq)
                DUMP("gX", gF[0][:], [128, FF], F32, t_f2)
                DUMP("fstX", STATS[:, 100:176], [128, 76], F32, t_f2)
            t_ev = None
            for r in range(4):
                sp = r % 2
                W("tensor", t_hq, stF["stg_free"][sp],
                  stF["h_free"][0], stF["h_free"][1], stF["h_free"][2])
                for b in range(8):
                    fcb = r * 8 + b
                    tok = O("tensor", lambda e, fcb=fcb, b=b, sp=sp: e.transpose(
                        psStgF[sp][:, b * 128:(b + 1) * 128],
                        hqF[:, fcb * 128:(fcb + 1) * 128], ident),
                        inc=sPE if b == 7 else None)
                W("vector", tok, stF["hqT_used"][tp] if r == 0 else None)
                t_ev = O("vector", lambda e, tp=tp, r=r, sp=sp: e.tensor_copy(
                    hqT[tp][:, r * 8:(r + 1) * 8, :],
                    psStgF[sp][:].rearrange("p (a b) -> p a b", a=8)), inc=sDV, flush_ap=hqT[tp][:, 0:1, 0:1])
                stF["stg_free"][sp] = t_ev
                stF["h_free"][sp] = t_ev
            stF["hq_free"] = t_ev
            t_hqT_ready[tp] = t_ev
            W("tensor", stF["t_wo"], stF["y_ps_free"][tp], stF["t_attnT"], t_b2r)
            for ob in range(2):
                for dc in range(8):
                    O("tensor", lambda e, tp=tp, dc=dc, ob=ob, i=i: e.matmul(
                        psY[tp][:, ob * 512:(ob + 1) * 512],
                        attnT[:, dc, i * 128:(i + 1) * 128],
                        wo_sb[:, dc, ob * 512:(ob + 1) * 512],
                        start=(dc == 0), stop=False))
            O("tensor", lambda e, tp=tp: e.matmul(
                psY[tp][:, 0:512], ones_bf[0:2, :], b2rows[:, 0:512], start=False, stop=False))
            O("tensor", lambda e, tp=tp: e.matmul(
                psY[tp][:, 512:1024], ones_bf[0:2, :], b2rows[:, 512:1024], start=False, stop=False))
        EM.raw("scalar", lambda e: e.preload_act_table(AF.Gelu))
        t_f2mm = None
        for fc in range(32):
            wpar = fc % 2
            W("sync", stF["w2_used"][wpar])
            t_w2 = DMA("sync", sSY, w2db[wpar], w2_p[fc])
            W("tensor", t_w2, t_hqT_ready[0] if fc == 0 else None,
              t_hqT_ready[1] if fc == 0 else None)
            for tp in range(2):
                for ob in range(2):
                    last = (tp == 1 and ob == 1)
                    tok = O("tensor", lambda e, tp=tp, fc=fc, ob=ob, wpar=wpar: e.matmul(
                        psY[tp][:, ob * 512:(ob + 1) * 512],
                        hqT[tp][:, fc, :], w2db[wpar][:, ob * 512:(ob + 1) * 512],
                        start=False, stop=(fc == 31)),
                        inc=sPE if last else None)
            t_f2mm = tok
            stF["w2_used"][wpar] = t_f2mm
        stF["hqT_used"][0] = t_f2mm
        stF["hqT_used"][1] = t_f2mm
        for tp in range(2):
            i = tiles[tp]
            W("vector", t_f2mm, stF["xF"][tp], stF["y_sb_free"][tp])
            t_y = O("vector", lambda e, tp=tp: e.tensor_add(yF[tp], xF[tp], psY[tp]), inc=sDV, flush_ap=yF[tp][:, 0:1])
            stF["y_ps_free"][tp] = t_y
            stF["xF_used"][tp] = t_y
            stF["xqF_used"][tp] = t_f2mm
            if s == 0 and g == 0 and tp == 0:
                DUMP("y0", yF[0][:], [128, D], F32, t_y)
            if s == 1 and g == 5 and tp == 0:
                DUMP("yX", yF[0][:], [128, D], F32, t_y)
                DUMP("xX", xF[0][:], [128, D], F32, t_y)
            W("gpsimd", t_y)
            t_st = DMA("gpsimd", sGQ, y_out[s, i * 128:(i + 1) * 128, :], yF[tp])
            stF["y_sb_free"][tp] = t_st
            y_last[0] = t_st
        stF["t_lastF_pe"] = t_f2mm

    seq_guard = None
    for s in range(NSEQ):
        if seq_guard is not None:
            for eng in Emitter.ENGS:
                W(eng, *seq_guard)

        W("sync", t_pro_gq)
        t_w = []
        for n in ("wq", "wk", "wv"):
            t_w.append(DMA("sync", sSY, wqkv_sb[n][:],
                           w_p[n].rearrange("a p b -> p a b")))
        EM.raw("scalar", lambda e: e.preload_act_table(AF.Sqrt))

        if s == 0:
            W("vector", t_pro_gq, (sDV, sDV.v))
        t_vset = O("vector", lambda e: e.memset(Vones[:].rearrange("p a b c -> p (a b c)"), 1.0), inc=sDV)
        st8 = dict(
            xb_free=[None, None], tmp_free=[None, None], xnT_free=[None, None],
            stage_free=[None, None], proj_psum_free=[None, None, None],
            rope_prev=[None, None], qr_free=None, kr_free=None,
            xnbf_free=None, xqbf_free=None, xnbf_spilled=None,
            xq_spill=[], t_w=t_w[-1], t_vset=t_vset,
        )
        st8["proj_psum_free"] = [t_w[0], t_w[1]]
        for i in range(NT):
            emit_E_tile(s, i, st8)
        t_E_dv = (sDV, sDV.v)
        t_E_gp = (sGP, sGP.v)
        if s == 0:
            DUMP("QT", QT, [128, 8, T], BF16, t_E_dv)
            DUMP("KT", KT, [128, 8, T], BF16, t_E_dv)
            DUMP("Von", Vones, [128, NT, H, 65], BF16, t_E_dv)

        EM.raw("scalar", lambda e: e.preload_act_table(AF.Exp))
        W("tensor", t_E_dv, t_E_gp)
        W("scalar", t_cst)
        stA = dict(
            pool_free={0: None, 1: None, 2: None},
            pt_free=[[None, None], [None, None]],
            pt_exp=[[None, None], [None, None]],
            pv_free=None, atmp_free=None, atmpbf_free=None, t_an=None,
        )
        for qb in range(4):
            for hp in range(8):
                emit_A_iter(qb, hp, stA)
        t_attnT_done = (sDV, sDV.v)
        t_A_pe = (sPE, sPE.v)
        if s == 0:
            DUMP("attnT", attnT, [128, 8, T], BF16, t_attnT_done)
        t_A_ac = (sAC, sAC.v)

        EM.raw("scalar", lambda e: e.preload_act_table(AF.Gelu))
        W("sync", t_A_pe, t_attnT_done, t_A_ac)
        t_wo = DMA("sync", sSY, wo_sb[:], w_p["wo"].rearrange("a p b -> p a b"))
        stF = dict(
            w1_used=[None, None], w2_used=[None, None],
            xqF=[None, None], xqF_used=[None, None],
            xF=[None, None], xF_used=[None, None],
            g_free=[None, None], hq_free=None, hqT_used=[None, None],
            y_ps_free=[None, None], y_sb_free=[None, None],
            h_free={0: None, 1: None, 2: None},
            stg_free=[None, None],
            t_wo=t_wo, t_attnT=t_attnT_done,
            first_gate=(sPE, sPE.v),
            t_lastF_pe=None,
        )
        stF["first_gate"] = None
        W("tensor", t_A_ac, t_attnT_done)
        W("gpsimd", t_A_pe, t_attnT_done, t_A_ac)
        for g in range(8):
            emit_F_group(s, g, stF)
            for eng in ("gpsimd", "sync", "tensor"):
                W(eng, *snap())
        seq_guard = snap()

    W("gpsimd", (sGQ, sGQ.v))
    EM.raw("gpsimd", lambda e: e.wait_ge(sGQ.h, sGQ.v))

    with nc.Block() as block:
        @block.sync
        def _(e):
            E = EngCtx(e)
            for fn in EM.ops["sync"]:
                fn(E)

        @block.tensor
        def _(e):
            E = EngCtx(e)
            for fn in EM.ops["tensor"]:
                fn(E)

        @block.vector
        def _(e):
            E = EngCtx(e)
            for fn in EM.ops["vector"]:
                fn(E)

        @block.scalar
        def _(e):
            E = EngCtx(e)
            for fn in EM.ops["scalar"]:
                fn(E)

        @block.gpsimd
        def _(e):
            E = EngCtx(e)
            for fn in EM.ops["gpsimd"]:
                fn(E)

    nc._dbg_names = list(dbg_tensors.keys())
    return nc


def _host_tables():
    pos = np.arange(T, dtype=np.float32)
    i = np.arange(HD // 2, dtype=np.float32)
    coss, sins = [], []
    for sc in (1.0, 4.0, 16.0):
        freqs = (1.0 / (10000.0 * sc) ** (2.0 * i / HD)).astype(np.float32)
        ang = pos[:, None] * freqs[None, :]
        coss.append(np.cos(ang).astype(np.float32))
        sins.append(np.sin(ang).astype(np.float32))
    cos = np.mean(np.stack(coss), 0).astype(np.float32)
    sin = np.mean(np.stack(sins), 0).astype(np.float32)
    return cos, sin


def kernel(L0_attn_x, L0_attn_Wq, L0_attn_Wk, L0_attn_Wv, L0_attn_Wo,
           L0_ffn_w1, L0_ffn_b1, L0_ffn_w2, L0_ffn_b2):
    x = np.ascontiguousarray(np.asarray(L0_attn_x, dtype=np.float32))
    cos, sin = _host_tables()
    ident = np.eye(128, dtype=np.float32).astype(ml_dtypes.bfloat16)
    shf = np.zeros((64, 128), dtype=np.float32)
    shf[np.arange(64), np.arange(64) + 64] = 1.0
    shf = shf.astype(ml_dtypes.bfloat16)
    cst = np.zeros((128, 8), dtype=np.float32)
    cst[:, 0] = EXP_BIAS
    cst[:, 1] = MAGIC

    common = {
        "wq": np.ascontiguousarray(L0_attn_Wq, dtype=np.float32),
        "wk": np.ascontiguousarray(L0_attn_Wk, dtype=np.float32),
        "wv": np.ascontiguousarray(L0_attn_Wv, dtype=np.float32),
        "wo": np.ascontiguousarray(L0_attn_Wo, dtype=np.float32),
        "w1": np.ascontiguousarray(L0_ffn_w1, dtype=np.float32),
        "b1": np.ascontiguousarray(np.asarray(L0_ffn_b1, dtype=np.float32).reshape(1, FF)),
        "w2": np.ascontiguousarray(L0_ffn_w2, dtype=np.float32),
        "b2": np.ascontiguousarray(np.asarray(L0_ffn_b2, dtype=np.float32).reshape(1, D)),
        "cos_t": cos, "sin_t": sin, "ident": ident, "shf": shf, "cst": cst,
    }
    nc = build_program()
    in_maps = []
    for c in range(NCORES):
        m = dict(common)
        m["x"] = np.ascontiguousarray(x[c * NSEQ:(c + 1) * NSEQ])
        in_maps.append(m)
    res = run_bass_kernel_spmd(nc, in_maps, core_ids=list(range(NCORES)))
    return np.concatenate([r["y"] for r in res.results], axis=0)


# version 14
# speedup vs baseline: 1.3929x; 1.3502x over previous
import numpy as np
import ml_dtypes
from contextlib import ExitStack

import concourse.bass as bass
import concourse.mybir as mybir
from concourse.bass_utils import run_bass_kernel_spmd

F32 = mybir.dt.float32
BF16 = mybir.dt.bfloat16
AF = mybir.ActivationFunctionType
ALU = mybir.AluOpType
AX = mybir.AxisListType

MAGIC = 1.5 * 2.0 ** 23
EXP_BIAS = -2.0
EPS = 1e-5
D, T, H, HD, FF = 1024, 2048, 16, 64, 4096
NT, NSEQ, NCORES = 16, 2, 8
DEBUG = False


class Sem:
    def __init__(self, nc, ctx, name):
        self.h = ctx.enter_context(nc.semaphore(name))
        self.v = 0


class Emitter:
    ENGS = ("tensor", "vector", "scalar", "gpsimd", "sync")

    def __init__(self):
        self.ops = {k: [] for k in self.ENGS}
        self.vec_sem = None
        self.sca_sem = None
        self.flush_dst_v = None
        self.flush_dst_s = None

    def wait(self, eng, *toks):
        toks = tuple(t for t in toks if t is not None)
        if not toks:
            return
        def fn(E, toks=toks):
            for t in toks:
                E.wait(t)
        self.ops[eng].append(fn)

    def op(self, eng, f, inc=None, n=1, flush_ap=None):
        if eng in ("vector", "scalar") and self.vec_sem is not None:
            sem = inc if inc is not None else self.vec_sem
            sem.v += n if inc is not None else 1
            tok = (sem, sem.v)
            def fn(E, f=f, sem=sem, v=tok[1]):
                f(E.e).then_inc(sem.h, v - (sem_prev[0] if False else 0) if False else (n if False else 1))
            def fn(E, f=f, sem=sem, nn=(n if inc is not None else 1)):
                f(E.e).then_inc(sem.h, nn)
            self.ops[eng].append(fn)
            self.wait(eng, tok)
            return tok
        if inc is None:
            self.ops[eng].append(lambda E, f=f: f(E.e))
            return None
        inc.v += n
        tok = (inc, inc.v)
        def fn(E, f=f, inc=inc, n=n):
            f(E.e).then_inc(inc.h, n)
        self.ops[eng].append(fn)
        return tok

    def dma(self, eng, sem, out, in_, **kw):
        sem.v += 16
        tok = (sem, sem.v)
        def fn(E, out=out, in_=in_, kw=kw, sem=sem):
            E.e.dma_start(out=out, in_=in_, **kw).then_inc(sem.h, 16)
        self.ops[eng].append(fn)
        return tok

    def raw(self, eng, f):
        self.ops[eng].append(lambda E, f=f: f(E.e))


class EngCtx:
    def __init__(self, e):
        self.e = e
        self.seen = {}

    def wait(self, tok):
        sem, val = tok
        if val <= 0 or self.seen.get(id(sem), -1) >= val:
            return
        self.e.wait_ge(sem.h, val)
        self.seen[id(sem)] = val


def build_program():
    nc = bass.Bass("TRN2", target_bir_lowering=False)
    ctx = ExitStack()
    nc._kernel_ctx = ctx
    dbg_tensors = {}

    x_in = nc.dram_tensor("x", [NSEQ, T, D], F32, kind="ExternalInput").ap()
    w_in = {n: nc.dram_tensor(n, [D, D], F32, kind="ExternalInput").ap()
            for n in ("wq", "wk", "wv", "wo")}
    w1_in = nc.dram_tensor("w1", [FF, D], F32, kind="ExternalInput").ap()
    b1_in = nc.dram_tensor("b1", [1, FF], F32, kind="ExternalInput").ap()
    w2_in = nc.dram_tensor("w2", [D, FF], F32, kind="ExternalInput").ap()
    b2_in = nc.dram_tensor("b2", [1, D], F32, kind="ExternalInput").ap()
    cos_in = nc.dram_tensor("cos_t", [T, 32], F32, kind="ExternalInput").ap()
    sin_in = nc.dram_tensor("sin_t", [T, 32], F32, kind="ExternalInput").ap()
    id_in = nc.dram_tensor("ident", [128, 128], BF16, kind="ExternalInput").ap()
    shf_in = nc.dram_tensor("shf", [64, 128], BF16, kind="ExternalInput").ap()
    cst_in = nc.dram_tensor("cst", [128, 8], F32, kind="ExternalInput").ap()
    y_out = nc.dram_tensor("y", [NSEQ, T, D], F32, kind="ExternalOutput").ap()
    w_p = {n: nc.dram_tensor(n + "_p", [8, 128, D], BF16, kind="Internal").ap()
           for n in ("wq", "wk", "wv", "wo")}
    w1_p = nc.dram_tensor("w1_p", [8, 128, FF], BF16, kind="Internal").ap()
    w2_p = nc.dram_tensor("w2_p", [32, 128, D], BF16, kind="Internal").ap()
    scl_p = nc.dram_tensor("scl_p", [4, 1], F32, kind="Internal").ap()
    rcp_p = nc.dram_tensor("rcp_p", [2, 512], F32, kind="Internal").ap()
    xq_p = nc.dram_tensor("xq_p", [8, 128, T], BF16, kind="Internal").ap()

    def arena(name, nbytes):
        return ctx.enter_context(nc.sbuf_tensor(name, [128, nbytes // 4], F32))

    AR_P = arena("arP", 17408)
    AR_B = arena("arB", 98816)
    AR_C = arena("arC", 49152)
    AR_D = arena("arD", 46080)

    def carve(ar, off, shape, dtype):
        nel = int(np.prod(shape[1:]))
        assert off % 4 == 0
        if dtype == BF16:
            assert nel % 2 == 0
            v = ar[:][0:shape[0], off // 4: off // 4 + nel // 2].bitcast(BF16)
        else:
            v = ar[:][0:shape[0], off // 4: off // 4 + nel]
        if len(shape) > 2:
            pat = " ".join(f"d{i}" for i in range(1, len(shape)))
            v = v.rearrange(f"p ({pat}) -> p {pat}",
                            **{f"d{i}": shape[i] for i in range(1, len(shape))})
        return v

    cos_tok = carve(AR_P, 0, [128, NT, 32], F32)
    sin_tok = carve(AR_P, 2048, [128, NT, 32], F32)
    cstf = carve(AR_P, 4096, [128, 8], F32)
    ident = carve(AR_P, 4128, [128, 128], BF16)
    shfI = carve(AR_P, 4384, [64, 128], BF16)
    ones_bf = carve(AR_P, 4640, [4, 128], BF16)
    onesrow_f = carve(AR_P, 4896, [1, 128], F32)
    onescol_f = carve(AR_P, 5408, [128, 1], F32)
    onescol_b = carve(AR_P, 5412, [128, 2], BF16)
    b1rows = carve(AR_P, 5440, [2, FF], BF16)
    b2rows = carve(AR_P, 13632, [2, D], BF16)
    STATS = carve(AR_P, 15680, [128, 176], F32)
    _so = [0]
    def stat(n=1):
        o = _so[0]; _so[0] += n
        assert _so[0] <= 176
        return STATS[:, o:o + n]

    QT = carve(AR_B, 0, [128, 8, T], BF16)
    KT = carve(AR_B, 32768, [128, 8, T], BF16)
    Vones = carve(AR_B, 65536, [128, NT, H, 65], BF16)
    gF = [carve(AR_B, 0, [128, FF], F32), carve(AR_B, 16384, [128, FF], F32)]
    hqF = carve(AR_B, 32768, [128, FF], BF16)
    hqT = [carve(AR_B, 40960, [128, 32, 128], BF16),
           carve(AR_B, 49152, [128, 32, 128], BF16)]
    xqF = [carve(AR_B, 57344, [128, 8, 128], BF16),
           carve(AR_B, 59392, [128, 8, 128], BF16)]
    xF = [carve(AR_B, 61440, [128, D], F32), carve(AR_B, 65536, [128, D], F32)]
    yF = [carve(AR_B, 69632, [128, D], F32), carve(AR_B, 73728, [128, D], F32)]
    sqscrapF = carve(AR_B, 77824, [128, 512], F32)
    wbufP = [carve(AR_B, 0, [128, 4096], F32), carve(AR_B, 16384, [128, 4096], F32)]
    wtmpP = carve(AR_B, 32768, [128, 4096], F32)
    woutP = carve(AR_B, 49152, [128, 4096], BF16)
    toutP = carve(AR_B, 57344, [128, 32, 128], BF16)
    browP = carve(AR_B, 65536, [1, FF], F32)
    blo_bf = carve(AR_B, 84992, [1, FF], BF16)
    absaccP = carve(AR_B, 81920, [128, 256], F32)
    red16P = carve(AR_B, 83968, [128, 16], F32)

    wqkv_sb = {n: carve(AR_C, i * 16384, [128, 8, D], BF16)
               for i, n in enumerate(("wq", "wk", "wv"))}
    PTb = [[carve(AR_C, (h * 2 + p) * 2048, [128, 1024], BF16) for p in range(2)]
           for h in range(2)]
    araw = [[carve(AR_C, 8192 + (p * 2 + h) * 2048, [65, 512], F32) for h in range(2)]
            for p in range(2)]
    arecip = [[carve(AR_C, 16384 + (p * 2 + h) * 2048, [1, 512], F32) for h in range(2)]
              for p in range(2)]
    repbuf = [[carve(AR_C, 24576 + (p * 2 + h) * 2048, [64, 512], F32) for h in range(2)]
              for p in range(2)]
    ntmp = [carve(AR_C, 32768 + p * 1024, [64, 512], BF16) for p in range(2)]
    w1db = [carve(AR_C, 0, [128, 8, 512], BF16), carve(AR_C, 8192, [128, 8, 512], BF16)]
    wo_sb = carve(AR_C, 16384, [128, 8, D], BF16)
    w2db = [carve(AR_C, 32768, [128, D], BF16), carve(AR_C, 34816, [128, D], BF16)]
    xbE = [carve(AR_D, 0, [128, D], F32), carve(AR_D, 4096, [128, D], F32)]
    tmpE = [carve(AR_D, 8192, [128, D], F32), carve(AR_D, 12288, [128, D], F32)]
    xnE = carve(AR_D, 16384, [128, D], F32)
    QfE = carve(AR_D, 20480, [128, D], F32)
    KfE = carve(AR_D, 24576, [128, D], F32)
    QrE = carve(AR_D, 28672, [128, D], BF16)
    KrE = carve(AR_D, 30720, [128, D], BF16)
    xnbfE = carve(AR_D, 32768, [128, D], BF16)
    xqbfE = carve(AR_D, 34816, [128, D], BF16)
    xnTE = [carve(AR_D, 36864, [128, 8, 128], BF16),
            carve(AR_D, 38912, [128, 8, 128], BF16)]
    tA = carve(AR_D, 40960, [128, 16, 32], F32)
    tB = carve(AR_D, 43008, [128, 16, 32], F32)
    attnT = carve(AR_D, 0, [128, 8, T], BF16)

    PS = ctx.enter_context(nc.psum_tensor("ps", [128, 4096], F32))
    def psv(off, shape, dtype=F32):
        w = shape[1] if dtype == F32 else shape[1] // 2
        v = PS[:][0:shape[0], off: off + w]
        return v.bitcast(BF16) if dtype == BF16 else v
    psQ, psK, psV = psv(0, [128, 1024]), psv(1024, [128, 1024]), psv(2048, [128, 1024])
    psStg = [psv(3072, [128, 1024], BF16), psv(3584, [128, 1024], BF16)]
    psPool = [psv(0, [128, 1024]), psv(1024, [128, 1024]), psv(2048, [128, 1024])]
    psPV = [psv(3072, [65, 512]), psv(3584, [65, 512])]
    psRep = psv(0, [64, 512])
    psShift = psv(512, [128, 512])
    psH = [psv(0, [128, 512]), psv(512, [128, 512]), psv(1024, [128, 512])]
    psStgF = [psv(0, [128, 1024], BF16), psv(512, [128, 1024], BF16)]
    psY = [psv(1536, [128, 1024]), psv(2560, [128, 1024])]

    nc.const_aps.register(F32, EXP_BIAS, cstf[:, 0:1])
    nc.const_aps.register(F32, MAGIC, cstf[:, 1:2])
    nc.const_aps.register(F32, EPS, cstf[:, 2:3])

    sSY = Sem(nc, ctx, "sSY"); sGQ = Sem(nc, ctx, "sGQ")
    sPE = Sem(nc, ctx, "sPE"); sDV = Sem(nc, ctx, "sDV")
    sAC = Sem(nc, ctx, "sAC"); sGP = Sem(nc, ctx, "sGP")
    ALLSEMS = (sSY, sGQ, sPE, sDV, sAC, sGP)

    EM = Emitter()
    EM.vec_sem = sDV
    EM.sca_sem = sAC
    FLS = carve(AR_P, 16704, [128, 4], F32)
    EM.flush_dst_v = FLS[:, 0:1]
    EM.flush_dst_s = FLS[:, 1:2]
    W, O, DMA = EM.wait, EM.op, EM.dma

    def DUMP(name, ap, shape, dtype, *toks):
        if not DEBUG or name in dbg_tensors:
            return
        dt_ = nc.dram_tensor("dbg_" + name, list(shape), dtype, kind="ExternalOutput").ap()
        dbg_tensors[name] = dt_
        W("gpsimd", *toks)
        tok = DMA("gpsimd", sGQ, dt_, ap)
        for eng in Emitter.ENGS:
            W(eng, tok)

    def snap():
        return [(s, s.v) for s in ALLSEMS]

    DMA("sync", sSY, cos_tok, cos_in.rearrange("(i p) c -> p i c", p=128))
    DMA("sync", sSY, sin_tok, sin_in.rearrange("(i p) c -> p i c", p=128))
    DMA("sync", sSY, cstf, cst_in)
    DMA("sync", sSY, ident, id_in)
    DMA("sync", sSY, shfI, shf_in)
    t_b1 = DMA("sync", sSY, browP, b1_in)
    t_cst = (sSY, sSY.v)

    W("vector", t_cst)
    O("vector", lambda e: e.memset(ones_bf[:], 1.0))
    O("vector", lambda e: e.memset(onesrow_f[:], 1.0))
    O("vector", lambda e: e.memset(onescol_f[:], 1.0))
    O("vector", lambda e: e.memset(onescol_b[:, 0:1], 1.0))

    W("vector", t_b1)
    O("vector", lambda e: e.tensor_copy(b1rows[0:1, :], browP[:]))
    O("vector", lambda e: e.tensor_sub(browP[:], browP[:], b1rows[0:1, :]))
    t_b1l = O("vector", lambda e: e.tensor_copy(blo_bf[:], browP[:]), inc=sDV, flush_ap=blo_bf[0:1, 0:1])
    W("gpsimd", t_b1l)
    t_b1r = DMA("gpsimd", sGQ, b1rows[1:2, :], blo_bf[:])
    W("sync", t_b1l)
    t_b2 = DMA("sync", sSY, browP[0:1, 0:D], b2_in)
    W("vector", t_b2, t_b1r)
    O("vector", lambda e: e.tensor_copy(b2rows[0:1, :], browP[0:1, 0:D]))
    O("vector", lambda e: e.tensor_sub(browP[0:1, 0:D], browP[0:1, 0:D], b2rows[0:1, :]))
    t_b2l = O("vector", lambda e: e.tensor_copy(blo_bf[0:1, 0:D], browP[0:1, 0:D]), inc=sDV, flush_ap=blo_bf[0:1, 0:1])
    W("gpsimd", t_b2l)
    t_b2r = DMA("gpsimd", sGQ, b2rows[1:2, :], blo_bf[0:1, 0:D])

    last_wout = [None, None]
    wbuf_cons = [None, None]
    for mi, name in enumerate(("wq", "wk", "wv", "wo")):
        for dc in range(8):
            par = (mi * 8 + dc) % 2
            dsthalf = woutP[:, par * 2048 + 0: par * 2048 + D]
            W("sync", wbuf_cons[par])
            tl = DMA("sync", sSY, wbufP[par][:, 0:D],
                     w_in[name][dc * 128:(dc + 1) * 128, :])
            W("vector", tl, last_wout[par])
            tc = O("vector", lambda e, par=par, dsthalf=dsthalf:
                   e.tensor_copy(dsthalf, wbufP[par][:, 0:D]), inc=sDV)
            wbuf_cons[par] = tc
            W("gpsimd", tc)
            last_wout[par] = DMA("gpsimd", sGQ, w_p[name][dc], dsthalf)

    prevl = [None, None]
    for fc in range(32):
        par = fc % 2
        W("sync", wbuf_cons[par])
        tl = DMA("sync", sSY, wbufP[par][:, 0:D], w1_in[fc * 128:(fc + 1) * 128, :])
        W("vector", tl, prevl[par], last_wout[par])
        last_wout[par] = None
        prevl[par] = O("vector", lambda e, fc=fc, par=par: e.tensor_reduce(
            absaccP[:, fc * 8:(fc + 1) * 8].unsqueeze(2),
            wbufP[par][:, 0:D].rearrange("p (a b) -> p a b", a=8), AX.X, ALU.add,
            apply_absolute_value=True), inc=sDV)
        wbuf_cons[par] = prevl[par]
    sum1 = stat(); sum_lo1 = stat()
    O("vector", lambda e: e.tensor_reduce(red16P[:].unsqueeze(2),
                                          absaccP[:].rearrange("p (a b) -> p a b", a=16),
                                          AX.X, ALU.add))
    t_s1 = O("vector", lambda e: e.tensor_reduce(sum1, red16P[:], AX.X, ALU.add), inc=sDV)
    DUMP("absacc1", absaccP[:], [128, 256], F32, t_s1)
    for dc in range(8):
        par = dc % 2
        W("sync", wbuf_cons[par])
        tl = DMA("sync", sSY, wbufP[par][:], w2_in[dc * 128:(dc + 1) * 128, :])
        W("vector", tl, prevl[par])
        prevl[par] = O("vector", lambda e, dc=dc, par=par: e.tensor_reduce(
            absaccP[:, dc * 8:(dc + 1) * 8].unsqueeze(2),
            wbufP[par][:].rearrange("p (a b) -> p a b", a=8), AX.X, ALU.add,
            apply_absolute_value=True), inc=sDV)
        wbuf_cons[par] = prevl[par]
    sum2 = stat()
    O("vector", lambda e: e.tensor_reduce(red16P[:, 0:8].unsqueeze(2),
                                          absaccP[:, 0:64].rearrange("p (a b) -> p a b", a=8),
                                          AX.X, ALU.add))
    t_s2 = O("vector", lambda e: e.tensor_reduce(sum2, red16P[:, 0:8], AX.X, ALU.add), inc=sDV)
    DUMP("absacc2", absaccP[:], [128, 256], F32, t_s2)
    DUMP("wbuf_w2last", wbufP[1][:], [128, 4096], F32, t_s2)

    s1hi = carve(AR_P, 16384, [128, 4], BF16)
    O("vector", lambda e: e.tensor_copy(s1hi[:, 0:1], sum1))
    O("vector", lambda e: e.tensor_sub(sum_lo1, sum1, s1hi[:, 0:1]))
    O("vector", lambda e: e.tensor_copy(s1hi[:, 1:2], sum_lo1))
    O("vector", lambda e: e.tensor_copy(s1hi[:, 2:3], sum2))
    O("vector", lambda e: e.tensor_sub(sum_lo1, sum2, s1hi[:, 2:3]))
    t_hilo = O("vector", lambda e: e.tensor_copy(s1hi[:, 3:4], sum_lo1), inc=sDV)
    W("tensor", t_s1, t_s2, t_hilo)
    O("tensor", lambda e: e.matmul(PS[0:1, 0:1], s1hi[:, 0:1], onescol_b[:, 0:1], start=True, stop=False))
    O("tensor", lambda e: e.matmul(PS[0:1, 0:1], s1hi[:, 1:2], onescol_b[:, 0:1], start=False, stop=True))
    O("tensor", lambda e: e.matmul(PS[0:1, 512:513], s1hi[:, 2:3], onescol_b[:, 0:1], start=True, stop=False))
    t_m = O("tensor", lambda e: e.matmul(PS[0:1, 512:513], s1hi[:, 3:4], onescol_b[:, 0:1], start=False, stop=True), inc=sPE)
    dqw1v, dqw2v, sw1v, sw2v = stat(), stat(), stat(), stat()
    W("vector", t_m)
    O("vector", lambda e: e.tensor_scalar(dqw1v[0:1, :], PS[0:1, 0:1], 1.0 / (FF * D), 1e-5, ALU.mult, ALU.max))
    O("vector", lambda e: e.tensor_scalar(dqw2v[0:1, :], PS[0:1, 512:513], 1.0 / (FF * D), 1e-5, ALU.mult, ALU.max))
    O("vector", lambda e: e.reciprocal(sw1v[0:1, :], dqw1v[0:1, :]))
    O("vector", lambda e: e.reciprocal(sw2v[0:1, :], dqw2v[0:1, :]))
    nr1, nr2 = stat(), stat()
    for (sv, dv_) in ((sw1v, dqw1v), (sw2v, dqw2v)):
        O("vector", lambda e, sv=sv, dv_=dv_: e.tensor_mul(nr1[0:1, :], dv_[0:1, :], sv[0:1, :]))
        O("vector", lambda e: e.tensor_scalar(nr1[0:1, :], nr1[0:1, :], -1.0, 2.0, ALU.mult, ALU.add))
        O("vector", lambda e, sv=sv: e.tensor_mul(sv[0:1, :], sv[0:1, :], nr1[0:1, :]))
    t_sc = O("vector", lambda e: e.tensor_copy(nr2[0:1, :], nr1[0:1, :]), inc=sDV)
    dqw1, dqw2, sw1, sw2 = stat(), stat(), stat(), stat()
    W("gpsimd", t_sc)
    t_up = None
    for idx, srcv in enumerate((dqw1v, dqw2v, sw1v, sw2v)):
        t_up = DMA("gpsimd", sGQ, scl_p[idx], srcv[0:1, :])
    W("gpsimd", t_up)
    for idx, dstv in enumerate((dqw1, dqw2, sw1, sw2)):
        t_screp = DMA("gpsimd", sGQ, dstv, scl_p[idx].to_broadcast((128, 1)))
    DUMP("prostats", STATS[:, 0:16], [128, 16], F32, t_screp)

    def quant_chunk(par, width, sw_ap, dq_ap):
        O("vector", lambda e, par=par, width=width, sw_ap=sw_ap: e.tensor_scalar(
            wtmpP[:, 0:width], wbufP[par][:, 0:width], sw_ap, MAGIC, ALU.mult, ALU.add))
        O("vector", lambda e, width=width: e.tensor_scalar(
            wtmpP[:, 0:width], wtmpP[:, 0:width], -MAGIC, -1.0, ALU.add, ALU.max))
        return O("vector", lambda e, width=width, dq_ap=dq_ap: e.tensor_scalar(
            woutP[:, 0:width], wtmpP[:, 0:width], 1.0, dq_ap, ALU.min, ALU.mult), inc=sDV, flush_ap=woutP[:, 0:1])

    prevl = [None, None]
    stage_cons = [None, None]
    prev_quant = None
    prev_tout_dma = None
    for fc in range(32):
        par = fc % 2
        W("sync", wbuf_cons[par])
        tl = DMA("sync", sSY, wbufP[par][:, 0:D], w1_in[fc * 128:(fc + 1) * 128, :])
        W("vector", tl, t_screp, prevl[par], prev_quant)
        tq = quant_chunk(par, D, sw1, dqw1)
        if fc == 0:
            DUMP("w1q0", woutP[:, 0:D], [128, D], BF16, tq)
        prevl[par] = tq
        wbuf_cons[par] = tq
        prev_quant = None
        sp = fc % 2
        W("tensor", tq, stage_cons[sp])
        for dc in range(8):
            last = (dc == 7)
            tok = O("tensor", lambda e, dc=dc, sp=sp: e.transpose(
                psStg[sp][:, dc * 128:(dc + 1) * 128],
                woutP[:, dc * 128:(dc + 1) * 128], ident),
                inc=sPE if last else None)
        t_tr = tok
        W("vector", t_tr, prev_tout_dma)
        t_ev = O("vector", lambda e, sp=sp: e.tensor_copy(
            toutP[:, 0:8, :], psStg[sp][:].rearrange("p (a b) -> p a b", a=8)), inc=sDV, flush_ap=toutP[:, 0:1, 0:1])
        stage_cons[sp] = t_ev
        prev_quant = t_tr
        W("gpsimd", t_ev)
        prev_tout_dma = DMA("gpsimd", sGQ, w1_p[:, :, fc * 128:(fc + 1) * 128].rearrange("a p b -> p a b"),
                            toutP[:, 0:8, :])
    for dc in range(8):
        par = dc % 2
        W("sync", wbuf_cons[par])
        tl = DMA("sync", sSY, wbufP[par][:], w2_in[dc * 128:(dc + 1) * 128, :])
        W("vector", tl, prevl[par], prev_quant)
        tq = quant_chunk(par, FF, sw2, dqw2)
        prevl[par] = tq
        wbuf_cons[par] = tq
        t_ev = None
        for r in range(4):
            sp = r % 2
            W("tensor", tq, stage_cons[sp])
            for b in range(8):
                fcb = r * 8 + b
                last = (b == 7)
                tok = O("tensor", lambda e, fcb=fcb, b=b, sp=sp: e.transpose(
                    psStg[sp][:, b * 128:(b + 1) * 128],
                    woutP[:, fcb * 128:(fcb + 1) * 128], ident),
                    inc=sPE if last else None)
            W("vector", tok, prev_tout_dma if r == 0 else None)
            t_ev = O("vector", lambda e, sp=sp, r=r: e.tensor_copy(
                toutP[:, r * 8:(r + 1) * 8, :],
                psStg[sp][:].rearrange("p (a b) -> p a b", a=8)), inc=sDV)
            stage_cons[sp] = t_ev
        prev_quant = tok
        W("gpsimd", t_ev)
        prev_tout_dma = DMA("gpsimd", sGQ, w2_p[:, :, dc * 128:(dc + 1) * 128].rearrange("a p b -> p a b"),
                            toutP[:])
    t_pro_gq = (sGQ, sGQ.v)

    cE = {k: [stat(), stat()] for k in
          ("mean", "var", "rstd", "c2", "nm", "amax", "f1", "dq1", "t0", "t1", "sq")}
    cE["pair"] = [stat(2), stat(2)]
    cE["sqp"] = [stat(2), stat(2)]
    cE["rsp"] = [stat(2), stat(2)]
    cF = {k: stat() for k in ("ms", "a", "s", "rc", "am2", "d2", "f2", "dq2", "u0", "u1")}
    cF["sqc"] = [stat(8), stat(8)]

    y_last = [None]

    def emit_E_tile(s, i, st8):
        par = i % 2
        W("gpsimd", st8["xb_free"][par])
        t_x = DMA("gpsimd", sGQ, xbE[par], x_in[s, i * 128:(i + 1) * 128, :])

        W("vector", t_x, st8["tmp_free"][par])
        O("vector", lambda e, par=par: e.tensor_reduce(
            cE["t0"][par], xbE[par][:], AX.X, ALU.add))
        O("vector", lambda e, par=par: e.scalar_tensor_tensor(
            tmpE[par][:], xbE[par][:], 1.0, xbE[par][:], ALU.mult, ALU.mult,
            accum_out=cE["sq"][par]))
        O("vector", lambda e, par=par: e.tensor_scalar(cE["mean"][par], cE["t0"][par], 1.0 / D, None, ALU.mult))
        O("vector", lambda e, par=par: e.tensor_scalar(cE["nm"][par], cE["t0"][par], -1.0 / D, None, ALU.mult))
        O("vector", lambda e, par=par: e.tensor_mul(cE["t0"][par], cE["mean"][par], cE["mean"][par]))
        O("vector", lambda e, par=par: e.scalar_tensor_tensor(
            cE["var"][par], cE["sq"][par], 1.0 / D, cE["t0"][par], ALU.mult, ALU.subtract))
        O("vector", lambda e, par=par: e.tensor_scalar(
            cE["pair"][par][:, 0:1], cE["var"][par], EPS, None, ALU.add))
        O("vector", lambda e, par=par: e.reciprocal(cE["t1"][par], cE["pair"][par][:, 0:1]))
        t_va = O("vector", lambda e, par=par: e.tensor_scalar(
            cE["pair"][par][:, 1:2], cE["var"][par], cE["t1"][par], EPS, ALU.mult, ALU.add), inc=sDV)
        W("scalar", t_va)
        t_sq1 = O("scalar", lambda e, par=par: e.activation(
            cE["sqp"][par], cE["pair"][par], AF.Sqrt), inc=sAC, flush_ap=cE["sqp"][par])
        W("vector", t_sq1)
        O("vector", lambda e, par=par: e.reciprocal(cE["rsp"][par], cE["sqp"][par]))
        O("vector", lambda e, par=par: e.tensor_copy(cE["rstd"][par], cE["rsp"][par][:, 0:1]))
        O("vector", lambda e, par=par: e.tensor_copy(cE["c2"][par], cE["rsp"][par][:, 1:2]))
        t_xn = O("vector", lambda e, par=par: e.tensor_scalar(
            xnE[:], xbE[par][:], cE["nm"][par], cE["rstd"][par], ALU.add, ALU.mult), inc=sDV)
        st8["xb_free"][par] = t_xn
        W("scalar", st8["xnbf_free"], t_xn)
        t_xnbf = O("scalar", lambda e: e.copy(xnbfE[:], xnE[:]), inc=sAC, flush_ap=xnbfE[:, 0:1])
        O("vector", lambda e, par=par: e.tensor_reduce(
            cE["amax"][par], xnE[:], AX.X, ALU.max, apply_absolute_value=True))
        O("vector", lambda e, par=par: e.tensor_scalar(
            cE["t0"][par], cE["c2"][par], cE["amax"][par], 1e-5, ALU.mult, ALU.max))
        O("vector", lambda e, par=par: e.tensor_scalar(cE["dq1"][par], cE["t0"][par], 1.0 / 127.0, None, ALU.mult))
        O("vector", lambda e, par=par: e.reciprocal(cE["t1"][par], cE["t0"][par]))
        O("vector", lambda e, par=par: e.tensor_scalar(
            cE["f1"][par], cE["t1"][par], cE["c2"][par], 127.0, ALU.mult, ALU.mult))
        O("vector", lambda e, par=par: e.tensor_scalar(
            tmpE[par][:], xnE[:], cE["f1"][par], MAGIC, ALU.mult, ALU.add))
        W("vector", st8["xqbf_free"])
        t_xq = O("vector", lambda e, par=par: e.tensor_scalar(
            xqbfE[:], tmpE[par][:], -MAGIC, cE["dq1"][par], ALU.add, ALU.mult), inc=sDV, flush_ap=xqbfE[:, 0:1])
        st8["tmp_free"][par] = t_xq
        if s == 0 and i == 0:
            DUMP("xn", xnE[:], [128, D], F32, t_xn)
            DUMP("xq", xqbfE[:], [128, D], BF16, t_xq)
            DUMP("estats", STATS[:, 16:48], [128, 32], F32, t_xq)

        W("tensor", t_xnbf, st8["stage_free"][0], st8["xnT_free"][par])
        for dc in range(8):
            tok = O("tensor", lambda e, dc=dc: e.transpose(
                psStg[0][:, dc * 128:(dc + 1) * 128], xnbfE[:, dc * 128:(dc + 1) * 128], ident),
                inc=sPE if dc == 7 else None)
        t_trxn = tok
        W("tensor", t_xq, st8["stage_free"][1])
        for dc in range(8):
            tok = O("tensor", lambda e, dc=dc: e.transpose(
                psStg[1][:, dc * 128:(dc + 1) * 128], xqbfE[:, dc * 128:(dc + 1) * 128], ident),
                inc=sPE if dc == 7 else None)
        t_trxq = tok
        st8["xqbf_free"] = t_trxq
        W("vector", t_trxn)
        t_xnT = O("vector", lambda e, par=par: e.tensor_copy(
            xnTE[par][:], psStg[0][:].rearrange("p (a b) -> p a b", a=8)), inc=sDV, flush_ap=xnTE[par][:, 0:1, 0:1])
        st8["stage_free"][0] = t_xnT
        W("vector", t_trxq, st8["xnbf_spilled"])
        t_xqT = O("vector", lambda e: e.tensor_copy(
            xnbfE[:].rearrange("p (a b) -> p a b", a=8),
            psStg[1][:].rearrange("p (a b) -> p a b", a=8)), inc=sDV, flush_ap=xnbfE[:, 0:1])
        st8["stage_free"][1] = t_xqT
        st8["xnbf_free"] = None
        W("gpsimd", t_xqT)
        t_sp = DMA("gpsimd", sGQ, xq_p[:, :, i * 128:(i + 1) * 128].rearrange("a p b -> p a b"),
                   xnbfE[:].rearrange("p (a b) -> p a b", a=8))
        st8["xq_spill"].append(t_sp)
        st8["xnbf_spilled"] = t_sp
        st8["xnbf_free"] = t_sp

        W("tensor", t_xnT, st8["t_w"], *st8["proj_psum_free"])
        for dc in range(8):
            st_, sp_ = (dc == 0), (dc == 7)
            for k, (ps_, nm) in enumerate(((psQ, "wq"), (psK, "wk"), (psV, "wv"))):
                for half in range(2):
                    last = sp_ and k == 2 and half == 1
                    tok = O("tensor", lambda e, dc=dc, ps_=ps_, nm=nm, half=half, st_=st_, sp_=sp_, par=par:
                            e.matmul(ps_[:, half * 512:(half + 1) * 512],
                                     xnTE[par][:, dc, :],
                                     wqkv_sb[nm][:, dc, half * 512:(half + 1) * 512],
                                     start=st_, stop=sp_),
                            inc=sPE if last else None)
        t_proj = tok
        st8["xnT_free"][par] = t_proj

        W("scalar", t_proj, *st8["rope_prev"])
        t_qf = O("scalar", lambda e: e.copy(QfE[:], psQ[:]), inc=sAC, flush_ap=QfE[:, 0:1])
        t_kf = O("scalar", lambda e: e.copy(KfE[:], psK[:]), inc=sAC, flush_ap=KfE[:, 0:1])
        W("scalar", t_proj, st8["t_vset"])
        t_vev = O("scalar", lambda e, i=i: e.copy(
            Vones[:, i, :, 0:64], psV[:].rearrange("p (h c) -> p h c", c=64)), inc=sAC, flush_ap=Vones[:, 0:1, 0:1, 0:1])
        st8["proj_psum_free"] = [t_qf, t_kf, t_vev]

        cosb = cos_tok[:, i, :].unsqueeze(1).to_broadcast((128, 16, 32))
        sinb = sin_tok[:, i, :].unsqueeze(1).to_broadcast((128, 16, 32))
        def rv(buf, parity):
            return buf.rearrange("p (h q t) -> p h q t", h=16, q=32)[:, :, :, parity]
        QE_, QO_ = rv(QfE[:], 0), rv(QfE[:], 1)
        KE_, KO_ = rv(KfE[:], 0), rv(KfE[:], 1)
        QrEv, QrOv = rv(QrE[:], 0), rv(QrE[:], 1)
        KrEv, KrOv = rv(KrE[:], 0), rv(KrE[:], 1)
        tAv, tBv = tA, tB
        W("vector", t_qf, st8["qr_free"])
        O("vector", lambda e, QE_=QE_, cosb=cosb, tAv=tAv: e.tensor_tensor(tAv, QE_, cosb, ALU.mult))
        O("vector", lambda e, QO_=QO_, sinb=sinb, tBv=tBv: e.tensor_tensor(tBv, QO_, sinb, ALU.mult))
        O("vector", lambda e, QrEv=QrEv, tAv=tAv, tBv=tBv: e.tensor_tensor(QrEv, tAv, tBv, ALU.subtract))
        O("vector", lambda e, QE_=QE_, sinb=sinb, tAv=tAv: e.tensor_tensor(tAv, QE_, sinb, ALU.mult))
        O("vector", lambda e, QO_=QO_, cosb=cosb, tBv=tBv: e.tensor_tensor(tBv, QO_, cosb, ALU.mult))
        t_ropeq = O("vector", lambda e, QrOv=QrOv, tAv=tAv, tBv=tBv: e.tensor_tensor(QrOv, tAv, tBv, ALU.add), inc=sDV, flush_ap=QrE[:, 0:1])
        W("gpsimd", t_kf, st8["kr_free"])
        O("gpsimd", lambda e, KrEv=KrEv, KE_=KE_, cosb=cosb: e.tensor_tensor(KrEv, KE_, cosb, ALU.mult))
        O("gpsimd", lambda e, KrOv=KrOv, KO_=KO_, sinb=sinb: e.tensor_tensor(KrOv, KO_, sinb, ALU.mult))
        O("gpsimd", lambda e, KrEv=KrEv, KrOv=KrOv: e.tensor_tensor(KrEv, KrEv, KrOv, ALU.subtract))
        O("gpsimd", lambda e, KrOv=KrOv, KO_=KO_, cosb=cosb: e.tensor_tensor(KrOv, KO_, cosb, ALU.mult))
        O("gpsimd", lambda e, KE_=KE_, sinb=sinb: e.tensor_tensor(KE_, KE_, sinb, ALU.mult))
        t_ropek = O("gpsimd", lambda e, KrOv=KrOv, KE_=KE_: e.tensor_tensor(KrOv, KrOv, KE_, ALU.add), inc=sGP)
        st8["rope_prev"] = [t_ropeq, t_ropek]
        if s == 0 and i == 0:
            DUMP("qf", QfE[:], [128, D], F32, t_qf)
            DUMP("qr", QrE[:], [128, D], BF16, t_ropeq)
            DUMP("kr", KrE[:], [128, D], BF16, t_ropek)

        W("tensor", t_ropeq, st8["stage_free"][0])
        for dc in range(8):
            tok = O("tensor", lambda e, dc=dc: e.transpose(
                psStg[0][:, dc * 128:(dc + 1) * 128], QrE[:, dc * 128:(dc + 1) * 128], ident),
                inc=sPE if dc == 7 else None)
        t_trq = tok
        st8["qr_free"] = t_trq
        W("tensor", t_ropek, st8["stage_free"][1])
        for dc in range(8):
            tok = O("tensor", lambda e, dc=dc: e.transpose(
                psStg[1][:, dc * 128:(dc + 1) * 128], KrE[:, dc * 128:(dc + 1) * 128], ident),
                inc=sPE if dc == 7 else None)
        t_trk = tok
        st8["kr_free"] = t_trk
        W("vector", t_trq)
        t_qtw = O("vector", lambda e, i=i: e.tensor_copy(
            QT[:, :, i * 128:(i + 1) * 128], psStg[0][:].rearrange("p (a b) -> p a b", a=8)), inc=sDV, flush_ap=QT[:, 0:1, 0:1])
        st8["stage_free"][0] = t_qtw
        W("vector", t_trk)
        t_ktw = O("vector", lambda e, i=i: e.tensor_copy(
            KT[:, :, i * 128:(i + 1) * 128], psStg[1][:].rearrange("p (a b) -> p a b", a=8)), inc=sDV, flush_ap=KT[:, 0:1, 0:1])
        st8["stage_free"][1] = t_ktw

    def A_scores(qb, hp, j, stA):
        p0, p1, pp = (2 * j) % 3, (2 * j + 1) % 3, j % 2
        W("tensor", stA["pool_free"][p0], stA["pool_free"][p1])
        tok = None
        for kc2 in range(2):
            kc = 2 * j + kc2
            for h in range(2):
                pool = psPool[p0 if h == 0 else p1]
                lo = 0 if h == 0 else 64
                last = (kc2 == 1 and h == 1)
                tok = O("tensor", lambda e, kc=kc, kc2=kc2, lo=lo, pool=pool, hp=hp, qb=qb:
                        e.matmul(pool[:, kc2 * 512:(kc2 + 1) * 512],
                                 KT[lo:lo + 64, hp, kc * 128:(kc + 1) * 128],
                                 QT[lo:lo + 64, hp, qb * 512:(qb + 1) * 512],
                                 start=True, stop=True, tile_position=(lo, 0)),
                        inc=sPE if last else None)
        t_sc_ = tok
        for h, pi in ((0, p0), (1, p1)):
            W("scalar", t_sc_, stA["pt_free"][h][pp])
            t_e = O("scalar", lambda e, h=h, pi=pi, pp=pp: e.activation(
                PTb[h][pp][:], psPool[pi][:], AF.Exp, bias=EXP_BIAS, scale=0.125), inc=sAC)
            stA["pool_free"][pi] = t_e
            stA["pt_exp"][h][pp] = t_e

    def A_pv(qb, hp, j, stA):
        pp = j % 2
        st_ = (j == 0)
        W("tensor", stA["pt_exp"][0][pp], stA["pt_exp"][1][pp],
          stA["pv_free"] if st_ else None)
        tok = None
        for h in range(2):
            for kc2 in range(2):
                kc = 2 * j + kc2
                h_abs = 2 * hp + h
                last = (h == 1 and kc2 == 1)
                tok = O("tensor", lambda e, h=h, kc=kc, kc2=kc2, h_abs=h_abs, pp=pp, st_=st_:
                        e.matmul(psPV[h][:],
                                 Vones[:, kc, h_abs, :],
                                 PTb[h][pp][:, kc2 * 512:(kc2 + 1) * 512],
                                 start=(st_ and kc2 == 0), stop=(kc == 15 and kc2 == 1)),
                        inc=sPE if last else None)
        t_pv_ = tok
        stA["pt_free"][0][pp] = t_pv_
        stA["pt_free"][1][pp] = t_pv_
        return t_pv_


    def A_norm_finish(stA):
        if stA["norm_pend"] is None:
            return
        qb, hp, kp, t_rc = stA["norm_pend"]
        stA["norm_pend"] = None
        W("gpsimd", t_rc)
        for h in range(2):
            DMA("gpsimd", sGQ, rcp_p[h], arecip[kp][h][:])
        t_bc = None
        for h in range(2):
            W("gpsimd", stA["rep_free"][kp][h])
            t_bc = DMA("gpsimd", sGQ, repbuf[kp][h][:], rcp_p[h].unsqueeze(0).to_broadcast((64, 512)))
        stA["rcp_free"][kp] = t_bc
        W("vector", t_bc)
        t_ev_ = O("vector", lambda e, qb=qb, hp=hp, kp=kp: e.tensor_mul(
            attnT[0:64, hp, qb * 512:(qb + 1) * 512], araw[kp][0][0:64, :], repbuf[kp][0][:]),
            inc=sDV, flush_ap=attnT[0:64, hp, qb * 512:qb * 512 + 1])
        stA["araw_free"][kp][0] = t_ev_
        stA["rep_free"][kp][0] = t_ev_
        W("vector", stA["ntmp_free"][kp])
        t_nt = O("vector", lambda e, kp=kp: e.tensor_mul(
            ntmp[kp][:], araw[kp][1][0:64, :], repbuf[kp][1][:]), inc=sDV,
            flush_ap=ntmp[kp][:, 0:1])
        stA["araw_free"][kp][1] = t_nt
        stA["rep_free"][kp][1] = t_nt
        W("gpsimd", t_nt)
        t_sh_ = DMA("gpsimd", sGQ, attnT[64:128, hp, qb * 512:(qb + 1) * 512], ntmp[kp][:])
        stA["ntmp_free"][kp] = t_sh_
        stA["t_last_norm"] = (t_ev_, t_sh_)

    def emit_A_iter(qb, hp, stA):
        A_scores(qb, hp, 0, stA)
        A_norm_finish(stA)
        t_pv_ = None
        for j in range(8):
            if j < 7:
                A_scores(qb, hp, j + 1, stA)
            t_pv_ = A_pv(qb, hp, j, stA)
        stA["kpar"] ^= 1
        kp = stA["kpar"]
        for h in range(2):
            W("scalar", t_pv_, stA["araw_free"][kp][h])
            t_ar = O("scalar", lambda e, h=h, kp=kp: e.copy(araw[kp][h][:], psPV[h][:]), inc=sAC,
                     flush_ap=araw[kp][h][:, 0:1])
            stA["araw_tok"][kp][h] = t_ar
        stA["pv_free"] = t_ar
        W("vector", stA["araw_tok"][kp][0], stA["araw_tok"][kp][1], stA["rcp_free"][kp])
        O("vector", lambda e, kp=kp: e.reciprocal(arecip[kp][0][:], araw[kp][0][64:65, :]),
          flush_ap=arecip[kp][0][:, 0:1])
        t_rc = O("vector", lambda e, kp=kp: e.reciprocal(arecip[kp][1][:], araw[kp][1][64:65, :]), inc=sDV,
            flush_ap=arecip[kp][1][:, 0:1])
        stA["norm_pend"] = (qb, hp, kp, t_rc)
        t_an = None
        stA["pv_free"] = t_an
        stA["t_an"] = t_an

    def emit_F_group(s, g, stF):
        tiles = [2 * g, 2 * g + 1]
        for tp, i in enumerate(tiles):
            W("gpsimd", stF["xqF_used"][tp])
            stF["xqF"][tp] = DMA("gpsimd", sGQ, xqF[tp][:],
                                 xq_p[:, :, i * 128:(i + 1) * 128].rearrange("a p b -> p a b"))
            W("gpsimd", stF["xF_used"][tp])
            stF["xF"][tp] = DMA("gpsimd", sGQ, xF[tp], x_in[s, i * 128:(i + 1) * 128, :])
        t_gelu_last = [None, None]
        for fb in range(8):
            wpar = fb % 2
            W("sync", stF["w1_used"][wpar])
            t_w1 = DMA("sync", sSY, w1db[wpar][:],
                       w1_p[:, :, fb * 512:(fb + 1) * 512].rearrange("a p b -> p a b"))
            for tp in range(2):
                hb = (fb * 2 + tp) % 3
                W("tensor", t_w1, stF["xqF"][tp], stF["h_free"][hb],
                  stF["first_gate"])
                stF["first_gate"] = None
                for dc in range(8):
                    O("tensor", lambda e, dc=dc, tp=tp, wpar=wpar, hb=hb: e.matmul(
                        psH[hb][:], xqF[tp][:, dc, :], w1db[wpar][:, dc, :],
                        start=(dc == 0), stop=False))
                t_h = O("tensor", lambda e, hb=hb, fb=fb: e.matmul(
                    psH[hb][:], ones_bf[0:2, :], b1rows[:, fb * 512:(fb + 1) * 512],
                    start=False, stop=True), inc=sPE)
                stF["w1_used"][wpar] = t_h
                if s == 0 and g == 0 and fb == 3 and tp == 0:
                    DUMP("w1db3", w1db[wpar][:], [128, 8, 512], BF16, t_w1)
                    DUMP("xqF0", xqF[0][:], [128, 8, 128], BF16, stF["xqF"][0])
                    DUMP("b1r", b1rows[:], [2, FF], BF16, t_h)
                if s == 0 and g == 0 and fb == 2 and tp == 0:
                    DUMP("w1db2", w1db[wpar][:], [128, 8, 512], BF16, t_w1)
                    DUMP("g2early", gF[0][:, 1024:1536], [128, 512], F32, t_gl)
                W("scalar", t_h, stF["g_free"][tp] if fb == 0 else None)
                t_gl = O("scalar", lambda e, tp=tp, fb=fb, hb=hb: e.activation(
                    gF[tp][:, fb * 512:(fb + 1) * 512], psH[hb][:], AF.Gelu), inc=sAC)
                stF["h_free"][hb] = t_gl
                t_sq_ = O("scalar", lambda e, tp=tp, fb=fb: e.activation(
                    sqscrapF[:], gF[tp][:, fb * 512:(fb + 1) * 512], AF.Square,
                    accum_out=cF["sqc"][tp][:, fb:fb + 1]), inc=sAC,
                    flush_ap=cF["sqc"][tp][:, fb:fb + 1])
                t_gelu_last[tp] = t_sq_
        for tp in range(2):
            i = tiles[tp]
            W("tensor", stF["t_wo"], stF["y_ps_free"][tp], stF["t_attnT"], stF["t_attnT_dma"], t_b2r)
            for ob in range(2):
                for dc in range(8):
                    O("tensor", lambda e, tp=tp, dc=dc, ob=ob, i=i: e.matmul(
                        psY[tp][:, ob * 512:(ob + 1) * 512],
                        attnT[:, dc, i * 128:(i + 1) * 128],
                        wo_sb[:, dc, ob * 512:(ob + 1) * 512],
                        start=(dc == 0), stop=False))
            O("tensor", lambda e, tp=tp: e.matmul(
                psY[tp][:, 0:512], ones_bf[0:2, :], b2rows[:, 0:512], start=False, stop=False))
            O("tensor", lambda e, tp=tp: e.matmul(
                psY[tp][:, 512:1024], ones_bf[0:2, :], b2rows[:, 512:1024], start=False, stop=False))
        t_hqT_ready = [None, None]
        EM.raw("scalar", lambda e: e.preload_act_table(AF.Sqrt))
        for tp in range(2):
            i = tiles[tp]
            W("vector", t_gelu_last[tp])
            O("vector", lambda e, tp=tp: e.tensor_reduce(
                cF["am2"], gF[tp][:], AX.X, ALU.max, apply_absolute_value=True))
            O("vector", lambda e, tp=tp: e.tensor_reduce(cF["ms"], cF["sqc"][tp][:], AX.X, ALU.add))
            t_a = O("vector", lambda e: e.tensor_scalar(cF["a"], cF["ms"], 1.0 / FF, None, ALU.mult), inc=sDV)
            W("scalar", t_a)
            t_sqf = O("scalar", lambda e: e.activation(cF["s"], cF["a"], AF.Sqrt, bias=EPS), inc=sAC,
                      flush_ap=cF["s"])
            W("vector", t_sqf)
            O("vector", lambda e: e.reciprocal(cF["rc"], cF["s"]))
            O("vector", lambda e: e.tensor_scalar(cF["d2"], cF["rc"], cF["am2"], 1e-5, ALU.mult, ALU.max))
            O("vector", lambda e: e.tensor_scalar(cF["dq2"], cF["d2"], 1.0 / 127.0, None, ALU.mult))
            O("vector", lambda e: e.reciprocal(cF["u0"], cF["d2"]))
            t_f2 = O("vector", lambda e: e.tensor_scalar(cF["f2"], cF["u0"], cF["rc"], 127.0, ALU.mult, ALU.mult), inc=sDV)
            if s == 0 and g == 0 and tp == 0:
                DUMP("g0", gF[0][:], [128, FF], F32, t_gelu_last[0])
                DUMP("fstats", STATS[:, 32:64], [128, 32], F32, t_f2)
            W("scalar", t_f2)
            t_p1 = O("scalar", lambda e, tp=tp: e.activation(
                gF[tp][:], gF[tp][:], AF.Copy, bias=MAGIC, scale=cF["f2"]), inc=sAC, flush_ap=gF[tp][:, 0:1])
            W("vector", t_p1, stF["hq_free"])
            t_hq = O("vector", lambda e, tp=tp: e.tensor_scalar(
                hqF[:], gF[tp][:], -MAGIC, cF["dq2"], ALU.add, ALU.mult), inc=sDV, flush_ap=hqF[:, 0:1])
            stF["g_free"][tp] = t_hq
            if s == 0 and g == 0 and tp == 0:
                DUMP("hq0", hqF[:], [128, FF], BF16, t_hq)
            if s == 1 and g == 5 and tp == 0:
                DUMP("hqX", hqF[:], [128, FF], BF16, t_hq)
                DUMP("gX", gF[0][:], [128, FF], F32, t_f2)
                DUMP("fstX", STATS[:, 100:176], [128, 76], F32, t_f2)
            t_ev = None
            for r in range(4):
                sp = r % 2
                W("tensor", t_hq, stF["stg_free"][sp],
                  stF["h_free"][0], stF["h_free"][1], stF["h_free"][2])
                for b in range(8):
                    fcb = r * 8 + b
                    tok = O("tensor", lambda e, fcb=fcb, b=b, sp=sp: e.transpose(
                        psStgF[sp][:, b * 128:(b + 1) * 128],
                        hqF[:, fcb * 128:(fcb + 1) * 128], ident),
                        inc=sPE if b == 7 else None)
                W("vector", tok, stF["hqT_used"][tp] if r == 0 else None)
                t_ev = O("vector", lambda e, tp=tp, r=r, sp=sp: e.tensor_copy(
                    hqT[tp][:, r * 8:(r + 1) * 8, :],
                    psStgF[sp][:].rearrange("p (a b) -> p a b", a=8)), inc=sDV, flush_ap=hqT[tp][:, 0:1, 0:1])
                stF["stg_free"][sp] = t_ev
                stF["h_free"][sp] = t_ev
            stF["hq_free"] = t_ev
            t_hqT_ready[tp] = t_ev
        EM.raw("scalar", lambda e: e.preload_act_table(AF.Gelu))
        t_f2mm = None
        for fc in range(32):
            wpar = fc % 2
            W("sync", stF["w2_used"][wpar])
            t_w2 = DMA("sync", sSY, w2db[wpar], w2_p[fc])
            W("tensor", t_w2, t_hqT_ready[0] if fc == 0 else None,
              t_hqT_ready[1] if fc == 0 else None)
            for tp in range(2):
                for ob in range(2):
                    last = (tp == 1 and ob == 1)
                    tok = O("tensor", lambda e, tp=tp, fc=fc, ob=ob, wpar=wpar: e.matmul(
                        psY[tp][:, ob * 512:(ob + 1) * 512],
                        hqT[tp][:, fc, :], w2db[wpar][:, ob * 512:(ob + 1) * 512],
                        start=False, stop=(fc == 31)),
                        inc=sPE if last else None)
            t_f2mm = tok
            stF["w2_used"][wpar] = t_f2mm
        stF["hqT_used"][0] = t_f2mm
        stF["hqT_used"][1] = t_f2mm
        for tp in range(2):
            i = tiles[tp]
            W("vector", t_f2mm, stF["xF"][tp], stF["y_sb_free"][tp])
            t_y = O("vector", lambda e, tp=tp: e.tensor_add(yF[tp], xF[tp], psY[tp]), inc=sDV, flush_ap=yF[tp][:, 0:1])
            stF["y_ps_free"][tp] = t_y
            stF["xF_used"][tp] = t_y
            stF["xqF_used"][tp] = t_f2mm
            if s == 0 and g == 0 and tp == 0:
                DUMP("y0", yF[0][:], [128, D], F32, t_y)
            if s == 1 and g == 5 and tp == 0:
                DUMP("yX", yF[0][:], [128, D], F32, t_y)
                DUMP("xX", xF[0][:], [128, D], F32, t_y)
            W("gpsimd", t_y)
            t_st = DMA("gpsimd", sGQ, y_out[s, i * 128:(i + 1) * 128, :], yF[tp])
            stF["y_sb_free"][tp] = t_st
            y_last[0] = t_st
        stF["t_lastF_pe"] = t_f2mm

    seq_guard = None
    for s in range(NSEQ):
        if seq_guard is not None:
            for eng in Emitter.ENGS:
                W(eng, *seq_guard)

        W("sync", t_pro_gq)
        t_w = []
        for n in ("wq", "wk", "wv"):
            t_w.append(DMA("sync", sSY, wqkv_sb[n][:],
                           w_p[n].rearrange("a p b -> p a b")))
        EM.raw("scalar", lambda e: e.preload_act_table(AF.Sqrt))

        if s == 0:
            W("vector", t_pro_gq, (sDV, sDV.v))
        t_vset = O("vector", lambda e: e.memset(Vones[:].rearrange("p a b c -> p (a b c)"), 1.0), inc=sDV)
        st8 = dict(
            xb_free=[None, None], tmp_free=[None, None], xnT_free=[None, None],
            stage_free=[None, None], proj_psum_free=[None, None, None],
            rope_prev=[None, None], qr_free=None, kr_free=None,
            xnbf_free=None, xqbf_free=None, xnbf_spilled=None,
            xq_spill=[], t_w=t_w[-1], t_vset=t_vset,
        )
        st8["proj_psum_free"] = [t_w[0], t_w[1]]
        for i in range(NT):
            emit_E_tile(s, i, st8)
        t_E_dv = (sDV, sDV.v)
        t_E_gp = (sGP, sGP.v)
        if s == 0:
            DUMP("QT", QT, [128, 8, T], BF16, t_E_dv)
            DUMP("KT", KT, [128, 8, T], BF16, t_E_dv)
            DUMP("Von", Vones, [128, NT, H, 65], BF16, t_E_dv)

        EM.raw("scalar", lambda e: e.preload_act_table(AF.Exp))
        W("tensor", t_E_dv, t_E_gp)
        W("scalar", t_cst)
        stA = dict(
            pool_free={0: None, 1: None, 2: None},
            pt_free=[[None, None], [None, None]],
            pt_exp=[[None, None], [None, None]],
            pv_free=None, kpar=0, norm_pend=None,
            araw_free=[[None, None], [None, None]],
            araw_tok=[[None, None], [None, None]],
            rcp_free=[None, None],
            rep_free=[[None, None], [None, None]],
            ntmp_free=[None, None],
            t_last_norm=None,
        )
        for qb in range(4):
            for hp in range(8):
                emit_A_iter(qb, hp, stA)
        A_norm_finish(stA)
        t_attnT_done = (sDV, sDV.v)
        t_attnT_dma = (sGQ, sGQ.v)
        t_A_pe = (sPE, sPE.v)
        if s == 0:
            DUMP("attnT", attnT, [128, 8, T], BF16, t_attnT_done)
        t_A_ac = (sAC, sAC.v)

        EM.raw("scalar", lambda e: e.preload_act_table(AF.Gelu))
        W("sync", t_A_pe, t_attnT_done, t_A_ac)
        t_wo = DMA("sync", sSY, wo_sb[:], w_p["wo"].rearrange("a p b -> p a b"))
        stF = dict(
            w1_used=[None, None], w2_used=[None, None],
            xqF=[None, None], xqF_used=[None, None],
            xF=[None, None], xF_used=[None, None],
            g_free=[None, None], hq_free=None, hqT_used=[None, None],
            y_ps_free=[None, None], y_sb_free=[None, None],
            h_free={0: None, 1: None, 2: None},
            stg_free=[None, None],
            t_wo=t_wo, t_attnT=t_attnT_done, t_attnT_dma=t_attnT_dma,
            first_gate=(sPE, sPE.v),
            t_lastF_pe=None,
        )
        stF["first_gate"] = None
        W("tensor", t_A_ac, t_attnT_done)
        W("gpsimd", t_A_pe, t_attnT_done, t_A_ac)
        for g in range(8):
            emit_F_group(s, g, stF)
            toks = [(x, x.v) for x in (sDV, sAC, sGP)]
            for eng in ("gpsimd", "sync"):
                W(eng, *toks)
        seq_guard = snap()

    W("gpsimd", (sGQ, sGQ.v))
    EM.raw("gpsimd", lambda e: e.wait_ge(sGQ.h, sGQ.v))

    with nc.Block() as block:
        @block.sync
        def _(e):
            E = EngCtx(e)
            for fn in EM.ops["sync"]:
                fn(E)

        @block.tensor
        def _(e):
            E = EngCtx(e)
            for fn in EM.ops["tensor"]:
                fn(E)

        @block.vector
        def _(e):
            E = EngCtx(e)
            for fn in EM.ops["vector"]:
                fn(E)

        @block.scalar
        def _(e):
            E = EngCtx(e)
            for fn in EM.ops["scalar"]:
                fn(E)

        @block.gpsimd
        def _(e):
            E = EngCtx(e)
            for fn in EM.ops["gpsimd"]:
                fn(E)

    nc._dbg_names = list(dbg_tensors.keys())
    return nc


def _host_tables():
    pos = np.arange(T, dtype=np.float32)
    i = np.arange(HD // 2, dtype=np.float32)
    coss, sins = [], []
    for sc in (1.0, 4.0, 16.0):
        freqs = (1.0 / (10000.0 * sc) ** (2.0 * i / HD)).astype(np.float32)
        ang = pos[:, None] * freqs[None, :]
        coss.append(np.cos(ang).astype(np.float32))
        sins.append(np.sin(ang).astype(np.float32))
    cos = np.mean(np.stack(coss), 0).astype(np.float32)
    sin = np.mean(np.stack(sins), 0).astype(np.float32)
    return cos, sin


def kernel(L0_attn_x, L0_attn_Wq, L0_attn_Wk, L0_attn_Wv, L0_attn_Wo,
           L0_ffn_w1, L0_ffn_b1, L0_ffn_w2, L0_ffn_b2):
    x = np.ascontiguousarray(np.asarray(L0_attn_x, dtype=np.float32))
    cos, sin = _host_tables()
    ident = np.eye(128, dtype=np.float32).astype(ml_dtypes.bfloat16)
    shf = np.zeros((64, 128), dtype=np.float32)
    shf[np.arange(64), np.arange(64) + 64] = 1.0
    shf = shf.astype(ml_dtypes.bfloat16)
    cst = np.zeros((128, 8), dtype=np.float32)
    cst[:, 0] = EXP_BIAS
    cst[:, 1] = MAGIC

    common = {
        "wq": np.ascontiguousarray(L0_attn_Wq, dtype=np.float32),
        "wk": np.ascontiguousarray(L0_attn_Wk, dtype=np.float32),
        "wv": np.ascontiguousarray(L0_attn_Wv, dtype=np.float32),
        "wo": np.ascontiguousarray(L0_attn_Wo, dtype=np.float32),
        "w1": np.ascontiguousarray(L0_ffn_w1, dtype=np.float32),
        "b1": np.ascontiguousarray(np.asarray(L0_ffn_b1, dtype=np.float32).reshape(1, FF)),
        "w2": np.ascontiguousarray(L0_ffn_w2, dtype=np.float32),
        "b2": np.ascontiguousarray(np.asarray(L0_ffn_b2, dtype=np.float32).reshape(1, D)),
        "cos_t": cos, "sin_t": sin, "ident": ident, "shf": shf, "cst": cst,
    }
    nc = build_program()
    in_maps = []
    for c in range(NCORES):
        m = dict(common)
        m["x"] = np.ascontiguousarray(x[c * NSEQ:(c + 1) * NSEQ])
        in_maps.append(m)
    res = run_bass_kernel_spmd(nc, in_maps, core_ids=list(range(NCORES)))
    return np.concatenate([r["y"] for r in res.results], axis=0)
